# version 26
# baseline (speedup 1.0000x reference)
import dataclasses
import numpy as np
import ml_dtypes
import concourse.bass as bass
import concourse.mybir as mybir
from concourse.bass_utils import run_bass_kernel_spmd
from contextlib import ExitStack

F32 = mybir.dt.float32
BF16 = mybir.dt.bfloat16
ALU = mybir.AluOpType
AF = mybir.ActivationFunctionType
AX = mybir.AxisListType

B, L, D, H = 16, 4096, 1024, 4096
NCORE = 8
NB = B // NCORE
TCORE = NB * L
NGRP = TCORE // 128
NTILE = TCORE // 512
OD, VQN, VQD = 64, 256, 64
NP = OD * (OD - 1) // 2
NPK = 126
LN_EPS = 1e-5
RMS_EPS = 1e-6


def bc(ap, parts, free):
    return dataclasses.replace(ap, ap=[[0, parts]] + list(free))


class Emitter:

    def __init__(self):
        self.counting = True
        self.cur = None
        self.handle = None
        self.ctr = {}
        self.events = {}
        self.waited = set()
        self.sems = {}
        self.dma_i = {"g": 0, "sy": 0}
        self.streams = {k: [] for k in ("t", "v", "s", "sy", "g")}
        self.last_inst_eng = None

    def reset_pass(self):
        self.dma_i = {"g": 0, "sy": 0}

    class _Inst:
        __slots__ = ("em", "inst", "dma_ctx")

        def __init__(self, em, inst, dma_ctx=None):
            self.em, self.inst, self.dma_ctx = em, inst, dma_ctx

        def inc(self, sem, n=1, ev=None):
            em = self.em
            if self.dma_ctx is not None:
                sem, val = self.dma_ctx
                if em.counting:
                    if ev is not None:
                        assert ev not in em.events, f"dup event {ev}"
                        em.events[ev] = (sem, val)
                    em.streams[em.last_inst_eng].append(("inc", sem, 16, ev))
                elif self.inst is not None:
                    self.inst.then_inc(em.sems[sem], 16)
                return self
            if em.counting:
                em.ctr[sem] = em.ctr.get(sem, 0) + n
                if ev is not None:
                    assert ev not in em.events, f"dup event {ev}"
                    em.events[ev] = (sem, em.ctr[sem])
                em.streams[em.last_inst_eng].append(("inc", sem, n, ev))
            elif self.inst is not None:
                self.inst.then_inc(em.sems[sem], n)
            return self

    class _Eng:
        __slots__ = ("_em", "_key")

        def __init__(self, em, key):
            self._em, self._key = em, key

        def wait(self, *evs):
            em = self._em
            for ev in evs:
                if em.counting:
                    em.waited.add(ev)
                    em.streams[self._key].append(("wait", ev))
                elif em.cur == self._key:
                    sem, val = em.events[ev]
                    em.handle.wait_ge(em.sems[sem], val)

        def __getattr__(self, op):
            em, key = self._em, self._key

            def f(*a, **kw):
                if op == "dma_start" and key in ("g", "sy"):
                    i = em.dma_i[key]
                    em.dma_i[key] = i + 1
                    if key == "g":
                        sem, val = f"g{i % 4}", 16 * (i // 4 + 1)
                        if em.counting and i >= 4:
                            em.streams[key].append(("waitv", f"g{i % 4}", 16 * (i // 4)))
                        if (not em.counting) and em.cur == key and i >= 4:
                            em.handle.wait_ge(em.sems[sem], 16 * (i // 4))
                    else:
                        sem, val = f"x{i % 8}", 16 * (i // 8 + 1)
                    if em.counting:
                        em.last_inst_eng = key
                        return Emitter._Inst(em, None, (sem, val))
                    if em.cur != key:
                        return Emitter._Inst(em, None, (sem, val))
                    return Emitter._Inst(em, getattr(em.handle, op)(*a, **kw),
                                         (sem, val))
                if em.counting:
                    em.last_inst_eng = key
                    return Emitter._Inst(em, None)
                if em.cur != key:
                    return Emitter._Inst(em, None)
                return Emitter._Inst(em, getattr(em.handle, op)(*a, **kw))

            return f

    def eng(self, key):
        return Emitter._Eng(self, key)


def build_program(debug=False):
    nc = bass.Bass("TRN2", target_bir_lowering=False)
    dt = nc.dram_tensor
    x_h = dt("x", [TCORE, D], F32, kind="ExternalInput").ap()
    out_h = dt("out", [TCORE, D], F32, kind="ExternalOutput").ap()
    w1_h = dt("w1", [H, D], F32, kind="ExternalInput").ap()
    w2_h = dt("w2", [D, H], F32, kind="ExternalInput").ap()
    b1_h = dt("b1", [1, H], F32, kind="ExternalInput").ap()
    b2_h = dt("b2", [1, D], F32, kind="ExternalInput").ap()
    opw_h = dt("opw", [NP, D], F32, kind="ExternalInput").ap()
    opb_h = dt("opb", [1, NP], F32, kind="ExternalInput").ap()
    oow_h = dt("oow", [D, OD], F32, kind="ExternalInput").ap()
    oob_h = dt("oob", [1, D], F32, kind="ExternalInput").ap()
    vqw_h = dt("vqw", [VQD, D], F32, kind="ExternalInput").ap()
    vqb_h = dt("vqb", [1, VQD], F32, kind="ExternalInput").ap()
    vqow_h = dt("vqow", [D, VQD], F32, kind="ExternalInput").ap()
    vqob_h = dt("vqob", [1, D], F32, kind="ExternalInput").ap()
    cb_h = dt("cb", [VQN, VQD], F32, kind="ExternalInput").ap()
    omix_h = dt("omix", [1, 1], F32, kind="ExternalInput").ap()
    vmix_h = dt("vmix", [1, 1], F32, kind="ExternalInput").ap()
    gam_h = dt("gam", [1, D], F32, kind="ExternalInput").ap()
    mconst_h = dt("mconst", [NP, OD * OD], BF16, kind="ExternalInput").ap()
    eyef_h = dt("eyef", [128, 128], F32, kind="ExternalInput").ap()
    eyeb_h = dt("eyeb", [128, 128], BF16, kind="ExternalInput").ap()

    s2t_scr = dt("s2t_scr", [H, D], BF16, kind="Internal").ap()
    row_scr = dt("row_scr", [16, D], F32, kind="Internal").ap()
    dbg = {}
    if debug:
        for nm, shp, dtp in [
            ("d_rstd", [128, NGRP], F32), ("d_negm", [128, NGRP], F32),
            ("d_alpha1", [128, 32], F32), ("d_a2rep", [1, D], F32),
            ("d_b2f", [1, D], BF16), ("d_hms", [2, D], F32),
            ("d_Gb", [64, 2 * OD], BF16), ("d_Crows", [1, 2 * D], BF16),
            ("d_vqin", [2, VQD], F32), ("d_scores", [2, VQN], F32),
            ("d_vqcode", [2, VQD], F32), ("d_hmT", [128, 16], F32),
            ("d_b1c", [128, 32], F32), ("d_vqcs", [2, D], F32),
            ("d_xnfm", [128, 8 * 512], BF16), ("d_gall", [128, 32 * 512], BF16),
            ("d_s1T", [128, H], BF16), ("d_s2r", [128, 2048], BF16),
            ("d_xrot", [64, 512], BF16),
            ("d_rowscr", [16, D], F32), ("d_sqc", [128, NGRP], F32),
            ("d_parT", [NPK, 32], BF16), ("d_omg", [64, 64], F32),
        ]:
            dbg[nm] = dt(nm, shp, dtp, kind="ExternalOutput").ap()

    ctx = ExitStack()

    def sb(name, shape, dtp):
        return ctx.enter_context(nc.sbuf_tensor(name, shape, dtp))

    s1T = sb("s1T", [128, 8 * H], BF16)
    g_all = sb("g_all", [128, 32 * 512], BF16)
    xn_fm = sb("xn_fm", [128, 8 * 512], BF16)
    xn_tmb = sb("xn_tmb", [128, 2 * D], BF16)
    x_ring = sb("x_ring", [128, 8 * D], F32)
    s2r = sb("s2r", [128, 3 * 2048], BF16)
    pre_buf = sb("pre_buf", [128, 4 * D], F32)
    out_part = sb("out_part", [128, 4 * D], F32)
    sqscr = sb("sqscr", [128, D], F32)
    x_rot_bf = sb("x_rot_bf", [65, 512], BF16)
    oowT = sb("oowT", [65, D], BF16)
    gamma_rep = sb("gamma_rep", [128, D], F32)
    alpha2_rep = sb("alpha2_rep", [128, D], F32)
    b2f_row = sb("b2f_row", [1, D], BF16)
    C_rowsb = sb("C_rowsb", [1, 2 * D], BF16)
    alpha1_c = sb("alpha1_c", [128, 32], F32)
    b1_c = sb("b1_c", [128, 32], F32)
    negm_c = sb("negm_c", [128, NGRP], F32)
    rstd_c = sb("rstd_c", [128, NGRP], F32)
    sq_c = sb("sq_c", [128, NGRP], F32)
    m2t = sb("m2t", [128, 1], F32)
    ms_t = sb("ms_t", [128, 4], F32)
    rstd2_t = sb("rstd2_t", [128, 4], F32)
    pa_b = sb("pa_b", [128, 8], BF16)
    onescol_b = sb("onescol_b", [128, 1], BF16)
    eyef = sb("eyef_s", [128, 128], F32)
    eyeb = sb("eyeb_s", [128, 128], BF16)
    ones1b = sb("ones1b", [1, 128], BF16)
    onesf2 = sb("onesf2", [1, 2], F32)
    konst = sb("konst", [128, 8], F32)
    alpha2c = sb("alpha2c", [128, 8], F32)
    b2c = sb("b2c", [128, 8], F32)
    b2fc = sb("b2fc", [128, 8], F32)
    om_rep = sb("om_rep", [128, 1], F32)
    vm_rep = sb("vm_rep", [128, 1], F32)
    ratio_rep = sb("ratio_rep", [128, 1], F32)
    rr_t = sb("rr_t", [1, 1], F32)
    hmT = sb("hmT", [128, 16], F32)
    hmTb = sb("hmTb", [128, 16], BF16)
    G_b = sb("G_b", [64, 2 * OD], BF16)
    paramsT = sb("paramsT", [NPK, 32], BF16)

    gaddr = nc.lookup_mloc(g_all).addr

    def ovA(name, shape, dtp, off):
        return nc.alloc_sbuf_tensor_at(name, shape, dtp, offset=gaddr + off)

    w1buf = ovA("w1buf", [128, 4 * D], F32, 0)
    w1sgn = ovA("w1sgn", [128, 4 * D], BF16, 16384)
    w2buf_a = ovA("w2buf_a", [128, H], F32, 0)
    w2buf_b = ovA("w2buf_b", [128, H], F32, 16384)
    opwT_full = ovA("opwT_full", [128, 8 * NP], BF16, 0)
    mbuf = ovA("mbuf", [NPK, 4 * OD * OD], BF16, 0)
    om_stage = ovA("om_stage", [2, OD * OD], F32, 0)

    paddr = nc.lookup_mloc(pre_buf).addr
    oaddr = nc.lookup_mloc(out_part).addr

    def ovP(name, shape, dtp, off):
        return nc.alloc_sbuf_tensor_at(name, shape, dtp, offset=paddr + off)

    def ovO(name, shape, dtp, off):
        return nc.alloc_sbuf_tensor_at(name, shape, dtp, offset=oaddr + off)

    w2sgnB = ovP("w2sgnB", [128, 2 * H], BF16, 0)
    crow_stage = ovP("crow_stage", [1, 2 * D], F32, 0)
    params_s = ovP("params_s", [2, NP], F32, 0)
    vqc_s = ovP("vqc_s", [2, D], F32, 0)
    oob_rep2 = ovP("oob_rep2", [2, D], F32, 4096)
    vqwT = ovP("vqwT", [128, 8 * VQD], F32, 8192)
    scores_s = ovP("scores_s", [2, VQN], F32, 8192)
    onehot = ovP("onehot", [2, VQN], F32, 9216)
    ncbT = ovP("ncbT", [64, VQN], F32, 10240)
    cb_s = ovP("cb_s", [128, 2 * VQD], F32, 11264)
    ncb = ovP("ncb", [128, 2 * VQD], F32, 11776)
    cbsq = ovP("cbsq", [128, VQD], F32, 12288)
    vq_in_s = ovP("vq_in_s", [2, VQD], F32, 12544)
    nvq = ovP("nvq", [2, VQD], F32, 12800)
    nvqT = ovP("nvqT", [64, 2], F32, 13056)
    vq_code_s = ovP("vq_code_s", [2, VQD], F32, 13088)
    vq_codeT = ovP("vq_codeT", [64, 2], F32, 13344)
    normt = ovP("normt", [128, 2], F32, 13376)
    vnorm = ovP("vnorm", [2, 1], F32, 13408)
    onehotT = ovP("onehotT", [128, 4], F32, 13440)
    omg = ovP("omg", [64, 64], F32, 13472)
    omg2 = ovP("omg2", [64, 64], F32, 13728)
    omg3 = ovP("omg3", [64, 64], F32, 13984)
    gt1 = ovP("gt1", [64, 64], F32, 14240)
    vqb_row = ovP("vqb_row", [1, VQD], F32, 14496)

    s2stage = ovO("s2stage", [128, 2 * 4096], BF16, 0)
    vqob_row = ovO("vqob_row", [1, D], F32, 0)
    opb_b = ovO("opb_b", [1, NP], BF16, 4096)
    hm_s = ovO("hm_s", [2, D], F32, 8192)
    vqpowT = ovO("vqpowT", [64, D], F32, 12288)

    smallbuf = nc.alloc_sbuf_tensor_at("smallbuf", [128, 2 * D], F32,
                                       offset=nc.lookup_mloc(s2r).addr)

    KB = [ctx.enter_context(nc.psum_tensor(f"K{i}", [128, 512], F32)) for i in range(8)]

    sems = {n: ctx.enter_context(nc.semaphore(f"sm_{n}"))
            for n in (["t", "v", "s", "s2f"]
                      + [f"g{i}" for i in range(4)]
                      + [f"x{i}" for i in range(8)])}

    E = Emitter()
    E.sems = sems

    def xsl(li):
        sl = li % 8
        return x_ring[:, D * sl: D * (sl + 1)]

    def banks(tt):
        if tt % 2 == 0:
            return (0, 1), (2, 3), [(4, 5), (6, 7), (0, 1), (2, 3)]
        return (4, 5), (6, 7), [(0, 1), (2, 3), (4, 5), (6, 7)]

    def program(E):
        t = E.eng("t"); v = E.eng("v"); s = E.eng("s"); sy = E.eng("sy"); g = E.eng("g")

        g.dma_start(out=eyef[:], in_=eyef_h[:, :]).inc("g", 16, "ld_eyef")
        g.dma_start(out=eyeb[:], in_=eyeb_h[:, :]).inc("g", 16, "ld_eyeb")
        pass
        g.dma_start(out=om_rep[:], in_=bc(omix_h, 128, [[1, 1]])).inc("g", 16, "ld_om")
        g.dma_start(out=vm_rep[:], in_=bc(vmix_h, 128, [[1, 1]])).inc("g", 16, "ld_vm")
        v.wait("ld_om", "ld_vm")
        v.reciprocal(rr_t[:, :], om_rep[0:1, 0:1])
        v.drain()
        v.tensor_tensor(rr_t[:, :], rr_t[:, :], vm_rep[0:1, 0:1], ALU.mult).inc(
            "v", 1, "ratio_ok")
        g.wait("ratio_ok")
        g.dma_start(out=row_scr[15:16, 0:1], in_=rr_t[:, :]).inc("g", 16, "rt_ratio")
        g.wait("rt_ratio")
        g.dma_start(out=ratio_rep[:], in_=bc(row_scr[15:16, 0:1], 128, [[1, 1]])).inc(
            "g", 16, "ld_ratio")
        g.dma_start(out=smallbuf[0:32, 0:128],
                    in_=b1_h.rearrange("o (r p) -> (o r) p", p=128)).inc("g", 16, "ld_b1")
        g.dma_start(out=smallbuf[0:8, 128:256],
                    in_=b2_h.rearrange("o (r p) -> (o r) p", p=128)).inc("g", 16, "ld_b2")

        t.wait("ld_b1", "ld_eyef")
        t.matmul(KB[2][0:128, 0:32], smallbuf[0:32, 0:128], eyef[0:32, 0:32],
                 start=True, stop=True).inc("t", 1, "b1_tr")
        t.wait("ld_b2")
        t.matmul(KB[3][0:128, 0:8], smallbuf[0:8, 128:256], eyef[0:8, 0:8],
                 start=True, stop=True).inc("t", 1, "b2_tr")
        v.wait("b1_tr")
        v.tensor_copy(b1_c[:], KB[2][0:128, 0:32]).inc("v", 1, "b1c_ev")
        v.wait("b2_tr")
        v.tensor_copy(b2c[:], KB[3][0:128, 0:8]).inc("v", 1, "b2c_ev")
        v.wait("ld_eyeb")
        v.memset(ones1b[:], 1.0)
        v.memset(onesf2[:], 1.0)
        v.memset(konst[:, 0:1], LN_EPS)
        v.memset(konst[:, 1:2], -0.5)
        v.memset(konst[:, 2:3], RMS_EPS)
        v.memset(konst[:, 3:4], 1.0 / D)
        v.memset(konst[:, 4:5], 0.5)
        v.memset(x_rot_bf[64:65, :], 1.0)
        v.memset(onescol_b[:], 1.0).inc("v", 1, "consts_v")

        def _pa_front(rr):
            if rr >= 2:
                for q in range(4):
                    sy.wait(f"xfA_{4*(rr-2)+q}")
            sy.dma_start(
                out=x_ring[:, D * ((4 * rr) % 8): D * ((4 * rr) % 8) + 4 * D].rearrange(
                    "p (gq d2) -> p gq d2", gq=4),
                in_=x_h[512 * rr: 512 * (rr + 1), :].rearrange(
                    "(gq p) d2 -> p gq d2", p=128)).inc("x", 16, f"ldx4_{rr}")
            for q in range(4):
                gg = 4 * rr + q
                v.wait(f"ldx4_{rr}")
                v.reduce_sum(negm_c[:, gg: gg + 1], xsl(gg), axis=AX.X)
                s.wait(f"ldx4_{rr}")
                if gg == 0:
                    s.wait("consts_v")
                s.drain()
                ins = s.activation(sqscr[:], xsl(gg), AF.Square,
                                   accum_out=sq_c[:, gg: gg + 1])
                if q == 3:
                    ins.inc("s", 1, f"pa_sq_r{rr}")

        def _pa_back(rr):
            sl = slice(4 * rr, 4 * rr + 4)
            v.drain()
            v.tensor_scalar(negm_c[:, sl], negm_c[:, sl], -1.0 / D, None, ALU.mult)
            v.wait(f"pa_sq_r{rr}")
            v.drain()
            v.tensor_tensor(rstd_c[:, sl], negm_c[:, sl], negm_c[:, sl], ALU.mult)
            v.drain()
            v.scalar_tensor_tensor(sq_c[:, sl], sq_c[:, sl], 1.0 / D,
                                   rstd_c[:, sl], ALU.mult, ALU.subtract).inc(
                "v", 1, f"pa_var_r{rr}")
            s.wait(f"pa_var_r{rr}")
            s.activation(rstd_c[:, sl], sq_c[:, sl], AF.Ln, bias=konst[:, 0:1])
            s.drain()
            s.activation(rstd_c[:, sl], rstd_c[:, sl], AF.Exp,
                         scale=konst[:, 1:2]).inc("s", 1, f"pa_rstd_r{rr}")
            for q in range(4):
                gg = 4 * rr + q
                if gg >= 4:
                    v.wait(f"xfA_{gg-4}")
                ins = v.tensor_copy(xn_fm[:, D * (gg % 4): D * (gg % 4 + 1)], xsl(gg))
                if q == 3:
                    ins.inc("v", 1, f"pa_xbr_{rr}")
            v.wait(f"pa_rstd_r{rr}")
            v.tensor_tensor(sq_c[:, sl], negm_c[:, sl], rstd_c[:, sl], ALU.mult)
            v.drain()
            v.tensor_copy(pa_b[:, 0:4], rstd_c[:, sl])
            v.tensor_copy(pa_b[:, 4:8], sq_c[:, sl]).inc("v", 1, f"pa_nmr_r{rr}")
            t.wait(f"pa_nmr_r{rr}", f"pa_xbr_{rr}")
            if rr == 0:
                t.wait("b1c_ev", "b2c_ev", "consts_v")
            for q in range(4):
                gg = 4 * rr + q
                bt = gg // 32
                st0 = (gg % 32) == 0
                sp0 = (gg % 32) == 31
                xb = xn_fm[:, D * (gg % 4): D * (gg % 4 + 1)]
                t.matmul(KB[4 + 2 * bt][0:1, :], pa_b[:, q: q + 1],
                         xb[:, 0:512], start=st0, stop=sp0)
                t.matmul(KB[5 + 2 * bt][0:1, :], pa_b[:, q: q + 1],
                         xb[:, 512:1024], start=st0, stop=sp0)
                t.matmul(KB[2 + bt][0:1, 0:1], pa_b[:, 4 + q: 5 + q], onescol_b[:],
                         start=st0, stop=sp0).inc("t", 1, f"xfA_{gg}")



        for i in range(32):
            if i % 2 == 1:
                _pa_front((i - 1) // 2)
                _pa_back((i - 1) // 2)
            if i >= 4:
                g.wait(f"w1sgn_{i-4}", f"w1abs_{i-4}")
            if i == 0:
                g.wait("b1_tr", "b2_tr")
            g.dma_start(out=w1buf[:, D * (i % 4): D * (i % 4 + 1)],
                        in_=w1_h[128 * i: 128 * (i + 1), :]).inc("g", 16, f"ld_w1_{i}")
            s.wait(f"ld_w1_{i}")
            s.sign(w1sgn[:, D * (i % 4): D * (i % 4 + 1)],
                   w1buf[:, D * (i % 4): D * (i % 4 + 1)]).inc("s", 1, f"w1sgn_{i}")
            v.wait(f"ld_w1_{i}")
            v.reduce_sum(alpha1_c[:, i: i + 1], w1buf[:, D * (i % 4): D * (i % 4 + 1)],
                         axis=AX.X, apply_absolute_value=True).inc("v", 1, f"w1abs_{i}")
            t.wait(f"w1sgn_{i}")
            if i == 0:
                t.wait("ld_eyeb")
            if i >= 1:
                t.wait(f"s1Tev_{i-1}_1")
            for c in range(8):
                bank = KB[0] if c < 4 else KB[1]
                ins = t.matmul(bank[:, 128 * (c % 4): 128 * (c % 4 + 1)],
                               w1sgn[:, D * (i % 4) + 128 * c: D * (i % 4) + 128 * (c + 1)],
                               eyeb[:], start=(c % 4 == 0), stop=(c % 4 == 3))
                if c in (3, 7):
                    ins.inc("t", 1, f"w1tr_{i}_{c // 4}")
            for hf in range(2):
                v.wait(f"w1tr_{i}_{hf}")
                dst = s1T[:].rearrange("p (c m) -> p c m", c=8)[
                    :, 4 * hf: 4 * hf + 4, 128 * i: 128 * (i + 1)]
                v.tensor_copy(dst, (KB[0] if hf == 0 else KB[1])[:].rearrange(
                    "p (c m) -> p c m", c=4)).inc("v", 1, f"s1Tev_{i}_{hf}")

        for bt in range(2):
            v.wait(f"xfA_{32*bt+31}")
            if bt == 0:
                v.wait("rt_ratio")
            if bt == 1:
                v.wait("hm_w_0")
            v.tensor_copy(rr_t[0:1, 0:1], KB[2 + bt][0:1, 0:1])
            v.drain()
            v.tensor_scalar(hm_s[0:1, 0:512], KB[4 + 2 * bt][0:1, :],
                            rr_t[0:1, 0:1], 1.0 / L, ALU.add, ALU.mult)
            v.tensor_scalar(hm_s[0:1, 512:1024], KB[5 + 2 * bt][0:1, :],
                            rr_t[0:1, 0:1], 1.0 / L, ALU.add, ALU.mult).inc(
                "v", 1, f"hm_row_{bt}")
            g.wait(f"hm_row_{bt}")
            g.dma_start(out=row_scr[2 + bt: 3 + bt, :], in_=hm_s[0:1, :]).inc(
                "g", 16, f"hm_w_{bt}")
        for pair in range(4):
            for hf in range(2):
                i = 2 * pair + hf
                wb = w2buf_a if hf == 0 else w2buf_b
                if pair >= 1:
                    sy.wait(f"w2sgn_{i-2}", f"w2abs_{i-2}")
                else:
                    sy.wait("w1sgn_31", "w1abs_31", "w1tr_31_1")
                sy.dma_start(out=wb[:], in_=w2_h[128 * i: 128 * (i + 1), :]).inc(
                    "x", 16, f"ld_w2_{i}")
                v.wait(f"ld_w2_{i}")
                v.reduce_sum(alpha2c[:, i: i + 1], wb[:], axis=AX.X,
                             apply_absolute_value=True).inc("v", 1, f"w2abs_{i}")
            s.wait(f"ld_w2_{2*pair}")
            if pair >= 1:
                s.wait(f"w2tr_{pair-1}_15")
            s.sign(w2sgnB[:, 0:2048], w2buf_a[:, 0:2048]).inc(
                "s", 1, f"w2q_{pair}_0")
            s.wait(f"ld_w2_{2*pair+1}")
            s.sign(w2sgnB[:, H: H + 2048], w2buf_b[:, 0:2048]).inc(
                "s", 1, f"w2q_{pair}_2")
            if pair >= 1:
                s.wait(f"w2tr_{pair-1}_31")
            s.sign(w2sgnB[:, 2048: 4096], w2buf_a[:, 2048:4096]).inc(
                "s", 1, f"w2sgn_{2*pair}")
            s.sign(w2sgnB[:, H + 2048: H + 4096], w2buf_b[:, 2048:4096]).inc(
                "s", 1, f"w2sgn_{2*pair+1}")
            t.wait(f"w2q_{pair}_0", f"w2q_{pair}_2")
            if pair == 0:
                t.wait("s1Tev_31_1", "hm_row_1")
            for m in range(32):
                bank = KB[m % 4]
                if m == 16:
                    t.wait(f"w2sgn_{2*pair}", f"w2sgn_{2*pair+1}")
                if pair > 0 or m >= 4:
                    pp, pm = (pair, m - 4) if m >= 4 else (pair - 1, 28 + m)
                    t.wait(f"s2ev_{pp}_{pm}")
                t.matmul(bank[:, 0:128], w2sgnB[:, 128 * m: 128 * (m + 1)],
                         eyeb[:], start=True, stop=False)
                t.matmul(bank[:, 128:256], w2sgnB[:, H + 128 * m: H + 128 * (m + 1)],
                         eyeb[:], start=False, stop=True).inc("t", 1, f"w2tr_{pair}_{m}")
                v.wait(f"w2tr_{pair}_{m}")
                slot = m // 16
                if pair == 0 and m == 16:
                    v.wait("hm_w_1")
                if pair > 0 or slot > 0:
                    ps_, pslot = (pair, slot - 1) if slot > 0 else (pair - 1, 1)
                    v.wait(f"s2wr_{ps_}_{pslot}")
                v.tensor_copy(
                    s2stage[:, 4096 * slot + 256 * (m % 16): 4096 * slot + 256 * (m % 16) + 256],
                    KB[m % 4][:, 0:256]).inc("v", 1, f"s2ev_{pair}_{m}")
                if m % 16 == 15:
                    g.wait(f"s2ev_{pair}_{m}")
                    dst = s2t_scr.rearrange("(mm p) d2 -> p mm d2", p=128)[
                        :, 16 * slot: 16 * (slot + 1), 256 * pair: 256 * (pair + 1)]
                    src = s2stage[:, 4096 * slot: 4096 * (slot + 1)].rearrange(
                        "p (mm d2) -> p mm d2", mm=16)
                    g.dma_start(out=dst, in_=src).inc("g", 16, f"s2wr_{pair}_{slot}")

        v.wait("w2abs_7")
        v.tensor_scalar(alpha2c[:], alpha2c[:], 1.0 / H, None, ALU.mult).inc("v", 1, "a2s")
        v.wait("b2c_ev", "a2s")
        v.reciprocal(b2fc[:], alpha2c[:])
        v.drain()
        v.tensor_tensor(b2fc[:], b2fc[:], b2c[:], ALU.mult).inc("v", 1, "b2f")
        v.wait("w1abs_31")
        v.tensor_scalar(alpha1_c[:], alpha1_c[:], 1.0 / D, None, ALU.mult).inc("v", 1, "a1s")
        t.wait("a2s", "b2f", "s2ev_3_31")
        t.matmul(KB[0][0:8, 0:128], alpha2c[:, :], eyef[:, :],
                 start=True, stop=True).inc("t", 1, "a2T_tr")
        t.matmul(KB[1][0:8, 0:128], b2fc[:, :], eyef[:, :],
                 start=True, stop=True).inc("t", 1, "b2fT_tr")
        v.wait("a2T_tr")
        v.tensor_copy(gamma_rep[0:8, 0:128], KB[0][0:8, 0:128]).inc("v", 1, "a2T_ev")
        v.wait("b2fT_tr")
        v.tensor_copy(gamma_rep[0:8, 128:256], KB[1][0:8, 0:128]).inc("v", 1, "b2fT_ev")
        g.wait("a2T_ev")
        g.dma_start(out=row_scr.rearrange("(o r) (q p) -> (o q) r p", r=1, p=128)[
            0:8, 0, :] if False else
            row_scr[0:1, :].rearrange("o (r p) -> (o r) p", p=128),
            in_=gamma_rep[0:8, 0:128]).inc("g", 16, "rt_a2w")
        g.wait("b2fT_ev")
        g.dma_start(out=row_scr[1:2, :].rearrange("o (r p) -> (o r) p", p=128),
                    in_=gamma_rep[0:8, 128:256]).inc("g", 16, "rt_b2w")
        g.wait("rt_a2w", "rt_b2w")
        g.dma_start(out=alpha2_rep[:], in_=bc(row_scr[0:1, :], 128, [[1, D]])).inc(
            "g", 16, "ld_a2rep")
        g.dma_start(out=gamma_rep[:], in_=bc(gam_h, 128, [[1, D]])).inc("g", 16, "ld_gam")
        g.wait("rt_b2w", "s2wr_3_1")
        g.dma_start(out=hm_s[0:1, 0:D], in_=row_scr[1:2, :]).inc("g", 16, "ld_b2fs")
        v.wait("ld_b2fs")
        v.tensor_copy(b2f_row[:], hm_s[0:1, 0:D]).inc("v", 1, "b2f_row_ok")

        for i in range(8):
            g.dma_start(out=smallbuf[:, 64 * i: 64 * (i + 1)],
                        in_=oow_h[128 * i: 128 * (i + 1), :]).inc("g", 16, f"ld_oow_{i}")
            g.dma_start(out=smallbuf[:, 512 + 64 * i: 512 + 64 * (i + 1)],
                        in_=vqow_h[128 * i: 128 * (i + 1), :]).inc("g", 16, f"ld_vqow_{i}")
        for i in range(8):
            t.wait(f"ld_oow_{i}")
            if i == 0:
                t.wait("w2tr_3_31", "ld_eyef", "a2T_ev", "b2fT_ev")
            if i > 1:
                t.wait(f"oowev_{i-2}")
            t.matmul(KB[i % 2][0:64, 0:128], smallbuf[:, 64 * i: 64 * (i + 1)],
                     eyef[:], start=True, stop=True).inc("t", 1, f"oowtr_{i}")
            v.wait(f"oowtr_{i}")
            v.tensor_copy(oowT[0:64, 128 * i: 128 * (i + 1)], KB[i % 2][0:64, 0:128]).inc(
                "v", 1, f"oowev_{i}")
        for i in range(8):
            t.wait(f"ld_vqow_{i}")
            if i == 0:
                t.wait("oowev_6", "oowev_7")
            if i > 1:
                t.wait(f"vqowev_{i-2}")
            t.matmul(KB[i % 2][0:64, 0:128], smallbuf[:, 512 + 64 * i: 512 + 64 * (i + 1)],
                     eyef[:], start=True, stop=True).inc("t", 1, f"vqowtr_{i}")
            v.wait(f"vqowtr_{i}")
            if i == 0:
                v.wait("s2ev_3_31")
            v.tensor_copy(vqpowT[0:64, 128 * i: 128 * (i + 1)], KB[i % 2][0:64, 0:128]).inc(
                "v", 1, f"vqowev_{i}")
        g.wait("vqowev_7")
        g.dma_start(out=smallbuf[0:64, 0:D], in_=vqw_h[:, :]).inc("g", 16, "ld_vqw")
        g.wait("w2tr_3_31")
        g.dma_start(out=vqb_row[:], in_=vqb_h[:, :]).inc("g", 16, "ld_vqb")
        for c in range(8):
            if c == 0:
                t.wait("ld_vqw", "vqowev_6", "vqowev_7")
            if c > 1:
                t.wait(f"vqwev_{c-2}")
            t.matmul(KB[c % 2][:, 0:64], smallbuf[0:64, 128 * c: 128 * (c + 1)],
                     eyef[0:64, 0:64], start=True, stop=True).inc("t", 1, f"vqwtr_{c}")
            v.wait(f"vqwtr_{c}")
            if c == 0:
                v.wait("w2sgn_7")
            v.tensor_copy(vqwT[:, 64 * c: 64 * (c + 1)], KB[c % 2][:, 0:64]).inc(
                "v", 1, f"vqwev_{c}")

        for i in range(16):
            q = g if i % 2 == 0 else sy
            if i >= 2:
                q.wait(f"opwtr_{i-2}_1")
            else:
                q.wait("vqwev_7", "w2sgn_7", "w2abs_7", "w2tr_3_31")
            q.dma_start(out=smallbuf[0:NPK, D * (i % 2): D * (i % 2) + D],
                        in_=opw_h[NPK * i: NPK * (i + 1), :]).inc("g", 16, f"ld_opw_{i}")
            t.wait(f"ld_opw_{i}")
            if i == 0:
                t.wait("vqwev_7")
            if i >= 1:
                t.wait(f"opwev_{i-1}_1")
            for c in range(8):
                bank = KB[0] if c < 4 else KB[1]
                ins = t.matmul(bank[:, 126 * (c % 4): 126 * (c % 4) + 126],
                               smallbuf[0:NPK, D * (i % 2) + 128 * c: D * (i % 2) + 128 * (c + 1)],
                               eyef[0:NPK, 0:NPK], start=(c % 4 == 0), stop=(c % 4 == 3))
                if c in (3, 7):
                    ins.inc("t", 1, f"opwtr_{i}_{c // 4}")
            for hf in range(2):
                v.wait(f"opwtr_{i}_{hf}")
                dst = opwT_full[:].rearrange("p (c q) -> p c q", c=8)[
                    :, 4 * hf: 4 * hf + 4, NPK * i: NPK * (i + 1)]
                v.tensor_copy(dst, (KB[0] if hf == 0 else KB[1])[:, 0:4 * NPK].rearrange(
                    "p (c q) -> p c q", q=NPK)).inc("v", 1, f"opwev_{i}_{hf}")
        g.wait("vqwev_7")
        g.dma_start(out=params_s[0:1, 0:NP], in_=opb_h[:, :]).inc("g", 16, "ld_opb")
        v.wait("ld_opb", "s2wr_3_1")
        v.tensor_copy(opb_b[:], params_s[0:1, 0:NP]).inc("v", 1, "opb_ok")

        g.wait("hm_w_1", "s2wr_3_1")
        g.dma_start(out=hm_s[0:2, :], in_=row_scr[2:4, :]).inc("g", 16, "hm_rt")


        for c in range(8):
            if c == 0:
                t.wait("hm_rt", "opwev_15_0", "opwev_15_1")
            if c > 1:
                t.wait(f"hmT_ev_{c-2}")
            t.matmul(KB[c % 2][0:128, 0:2], hm_s[0:2, 128 * c: 128 * (c + 1)],
                     eyef[0:2, 0:2], start=True, stop=True).inc("t", 1, f"hmT_tr_{c}")
            v.wait(f"hmT_tr_{c}")
            v.tensor_copy(hmT[:, 2 * c: 2 * c + 2], KB[c % 2][0:128, 0:2]).inc(
                "v", 1, f"hmT_ev_{c}")
        v.drain()
        v.tensor_copy(hmTb[:], hmT[:]).inc("v", 1, "hmTb_ok")

        for n in range(4):
            for c in range(8):
                if n == 0 and c == 0:
                    t.wait("hmTb_ok", "opb_ok")
                t.matmul(KB[n][0:2, 0:504], hmTb[:, 2 * c: 2 * c + 2],
                         opwT_full[:].rearrange("p (cc q) -> p cc q", cc=8)[
                             :, c, 504 * n: 504 * (n + 1)],
                         start=(c == 0), stop=False)
            t.matmul(KB[n][0:2, 0:504], ones1b[0:1, 0:2],
                     opb_b[0:1, 504 * n: 504 * (n + 1)], start=False, stop=True).inc(
                "t", 1, f"par_{n}")
        for n in range(4):
            v.wait(f"par_{n}")
            if n == 0:
                v.wait("opwev_15_1")
            v.tensor_copy(params_s[0:2, 504 * n: 504 * (n + 1)], KB[n][0:2, 0:504]).inc(
                "v", 1, f"par_ev_{n}")
        for kc in range(16):
            if kc == 0:
                t.wait("par_ev_3")
            t.matmul(KB[4][0:NPK, 2 * kc: 2 * kc + 2],
                     params_s[0:2, NPK * kc: NPK * (kc + 1)], eyef[0:2, 0:2],
                     start=(kc == 0), stop=(kc == 15), skip_group_check=True).inc(
                "t", 1, f"parT_{kc}")
        v.wait("parT_15")
        v.tensor_copy(paramsT[:, :], KB[4][0:NPK, 0:32]).inc("v", 1, "parT_ev")
        v.wait("ldxm_0")
        for jj0 in range(2):
            v.tensor_scalar(xn_tmb[:, D * jj0: D * (jj0 + 1)], xsl(64 + jj0),
                            negm_c[:, jj0: jj0 + 1], rstd_c[:, jj0: jj0 + 1],
                            ALU.add, ALU.mult).inc("v", 1, f"v_xn_0_{jj0}")

        for kc in range(16):
            q = g if kc % 2 == 0 else sy
            if kc >= 4:
                q.wait(f"om_kc_{kc-4}")
            else:
                q.wait("par_3")
            q.dma_start(out=mbuf[:, 4096 * (kc % 4): 4096 * (kc % 4) + 4096],
                        in_=mconst_h[NPK * kc: NPK * (kc + 1), :]).inc(
                "g", 16, f"ld_m_{kc}")
            t.wait(f"ld_m_{kc}")
            if kc == 0:
                t.wait("parT_ev")
            ins = None
            for n in range(8):
                ins = t.matmul(KB[n][0:2, :], paramsT[:, 2 * kc: 2 * kc + 2],
                               mbuf[:, 4096 * (kc % 4) + 512 * n:
                                    4096 * (kc % 4) + 512 * (n + 1)],
                               start=(kc == 0), stop=(kc == 15), skip_group_check=True)
            ins.inc("t", 1, f"om_kc_{kc}")
        for n in range(8):
            if n == 0:
                v.wait("om_kc_15")
            v.tensor_copy(om_stage[0:2, 512 * n: 512 * (n + 1)],
                          KB[n][0:2, :]).inc("v", 1, f"om_ev_{n}")
        g.wait("om_ev_7")
        g.dma_start(out=row_scr[4:12, :].rearrange("(b rr) q -> b (rr q)", b=2),
                    in_=om_stage[0:2, :]).inc("g", 16, "om_w")

        for btch in range(2):
            g.wait("om_w")
            if btch == 1:
                g.wait("G_done_0")
            src = row_scr[4 + 4 * btch: 8 + 4 * btch, :].rearrange(
                "r (q d2) -> (r q) d2", d2=64)
            g.dma_start(out=omg[:, :], in_=src).inc("g", 16, f"ld_omg_{btch}")
            v.wait(f"ld_omg_{btch}")
            v.nop() if False else None
            t.wait(f"ld_omg_{btch}")
            t.matmul(KB[0][0:64, 0:64], omg[:], omg[:], start=True, stop=True).inc(
                "t", 1, f"om2_{btch}")
            v.wait(f"om2_{btch}")
            v.tensor_copy(omg2[:], KB[0][0:64, 0:64]).inc("v", 1, f"om2ev_{btch}")
            t.wait(f"om2ev_{btch}")
            t.matmul(KB[0][0:64, 0:64], omg2[:], omg[:], start=True, stop=True).inc(
                "t", 1, f"om3_{btch}")
            v.wait(f"om3_{btch}")
            v.tensor_copy(omg3[:], KB[0][0:64, 0:64])
            v.drain()
            v.scalar_tensor_tensor(gt1[:], omg2[:], -0.5, omg[:], ALU.mult, ALU.add)
            v.drain()
            v.scalar_tensor_tensor(gt1[:], omg3[:], -1.0 / 6.0, gt1[:], ALU.mult, ALU.add)
            v.drain()
            v.tensor_tensor(gt1[:], gt1[:], eyef[0:64, 0:64], ALU.add)
            v.drain()
            v.tensor_copy(G_b[0:64, 64 * btch: 64 * (btch + 1)], gt1[:]).inc(
                "v", 1, f"G_done_{btch}")

        for c in range(8):
            if c == 0:
                t.wait("om3_1", "hmT_ev_7", "vqwev_7", "consts_v")
            t.matmul(KB[4][0:2, 0:VQD], hmT[:, 2 * c: 2 * c + 2],
                     vqwT[:, 64 * c: 64 * (c + 1)], start=(c == 0), stop=False)
        t.matmul(KB[4][0:2, 0:VQD], onesf2[:, :], vqb_row[:, :],
                 start=False, stop=True).inc("t", 1, "vqin_mm")
        v.wait("vqin_mm")
        v.tensor_copy(vq_in_s[:, :], KB[4][0:2, 0:VQD]).inc("v", 1, "vqin_ev")
        s.wait("vqin_ev")
        s.activation(nvq[:, :], vq_in_s[:, :], AF.Square, accum_out=vnorm[:, :])
        s.drain()
        s.activation(vnorm[:, :], vnorm[:, :], AF.Ln)
        s.drain()
        s.activation(vnorm[:, :], vnorm[:, :], AF.Exp, scale=konst[0:2, 4:5]).inc(
            "s", 1, "vqnorm")
        v.wait("vqnorm")
        v.tensor_scalar_max(vnorm[:, :], vnorm[:, :], 1e-12)
        v.drain()
        v.reciprocal(vnorm[:, :], vnorm[:, :])
        v.drain()
        v.tensor_scalar(nvq[:, :], vq_in_s[:, :], vnorm[:, 0:1], None, ALU.mult).inc(
            "v", 1, "nvq_ok")
        g.wait("ld_vqb", "om_w")
        g.dma_start(out=cb_s[:].rearrange("p (u d2) -> p u d2", u=2),
                    in_=cb_h.rearrange("(u p) d2 -> p u d2", p=128)).inc("g", 16, "ld_cb")
        s.wait("ld_cb")
        s.activation(cbsq[:, 0:VQD], cb_s[:, 0:VQD], AF.Square, accum_out=normt[:, 0:1])
        s.drain()
        s.activation(cbsq[:, 0:VQD], cb_s[:, VQD: 2 * VQD], AF.Square,
                     accum_out=normt[:, 1:2])
        s.drain()
        s.activation(normt[:, :], normt[:, :], AF.Ln)
        s.drain()
        s.activation(normt[:, :], normt[:, :], AF.Exp, scale=konst[:, 4:5]).inc(
            "s", 1, "cbnorm")
        v.wait("cbnorm")
        v.tensor_scalar_max(normt[:, :], normt[:, :], 1e-12)
        v.drain()
        v.reciprocal(normt[:, :], normt[:, :])
        v.drain()
        v.tensor_scalar(ncb[:, 0:VQD], cb_s[:, 0:VQD], normt[:, 0:1], None, ALU.mult)
        v.tensor_scalar(ncb[:, VQD: 2 * VQD], cb_s[:, VQD: 2 * VQD], normt[:, 1:2],
                        None, ALU.mult).inc("v", 1, "ncb_ok")
        t.wait("nvq_ok")
        t.matmul(KB[5][0:64, 0:2], nvq[0:2, 0:VQD], eyef[0:2, 0:2],
                 start=True, stop=True).inc("t", 1, "nvqT_tr")
        v.wait("nvqT_tr")
        v.tensor_copy(nvqT[:, :], KB[5][0:64, 0:2]).inc("v", 1, "nvqT_ev")
        for u in range(2):
            if u == 0:
                t.wait("ncb_ok", "nvqT_ev")
            else:
                t.wait(f"ncbT_ev_0")
            t.matmul(KB[5][0:64, 0:128], ncb[:, VQD * u: VQD * (u + 1)],
                     eyef[:, 0:128], start=True, stop=True).inc("t", 1, f"ncbT_tr_{u}")
            v.wait(f"ncbT_tr_{u}")
            v.tensor_copy(ncbT[0:64, 128 * u: 128 * (u + 1)], KB[5][0:64, 0:128]).inc(
                "v", 1, f"ncbT_ev_{u}")
        t.wait("ncbT_ev_1")
        t.matmul(KB[4][0:2, 0:VQN], nvqT[0:64, 0:2], ncbT[0:64, 0:VQN],
                 start=True, stop=True).inc("t", 1, "scores_mm")
        v.wait("scores_mm")
        v.tensor_copy(scores_s[:, :], KB[4][0:2, 0:VQN])
        v.drain()
        v.reduce_max(vnorm[:, :], scores_s[:, :], axis=AX.X)
        v.drain()
        v.tensor_scalar(onehot[:, :], scores_s[:, :], vnorm[:, 0:1], None,
                        ALU.is_ge).inc("v", 1, "onehot_ok")
        for u in range(2):
            if u == 0:
                t.wait("onehot_ok")
            t.matmul(KB[5][0:128, 2 * u: 2 * u + 2], onehot[0:2, 128 * u: 128 * (u + 1)],
                     eyef[0:2, 0:2], start=(u == 0), stop=(u == 1)).inc(
                "t", 1, f"ohT_tr_{u}")
        v.wait("ohT_tr_1")
        v.tensor_copy(onehotT[:, 0:4], KB[5][0:128, 0:4]).inc("v", 1, "ohT_ev")
        for u in range(2):
            if u == 0:
                t.wait("ohT_ev")
            t.matmul(KB[4][0:2, 0:VQD], onehotT[:, 2 * u: 2 * u + 2],
                     cb_s[:, VQD * u: VQD * (u + 1)], start=(u == 0), stop=(u == 1)).inc(
                "t", 1, f"vqcode_mm_{u}")
        v.wait("vqcode_mm_1")
        v.tensor_copy(vq_code_s[:, :], KB[4][0:2, 0:VQD]).inc("v", 1, "vqcode_ev")
        t.wait("vqcode_ev")
        t.matmul(KB[5][0:64, 0:2], vq_code_s[0:2, 0:VQD], eyef[0:2, 0:2],
                 start=True, stop=True).inc("t", 1, "vqcT_tr")
        v.wait("vqcT_tr")
        v.tensor_copy(vq_codeT[:, :], KB[5][0:64, 0:2]).inc("v", 1, "vqcT_ev")
        g.wait("parT_ev", "s2wr_3_1")
        g.dma_start(out=vqob_row[:, :], in_=vqob_h[:, :]).inc("g", 16, "ld_vqob")
        g.dma_start(out=oob_rep2[:, :], in_=bc(oob_h, 2, [[1, D]])).inc("g", 16, "ld_oob")
        for b in range(2):
            bk0, bk1 = (6, 7) if b == 0 else (4, 5)
            if b == 0:
                t.wait("vqcT_ev", "ld_vqob")
            for half, bk in ((0, bk0), (1, bk1)):
                t.matmul(KB[bk][0:1, :], vq_codeT[:, b: b + 1],
                         vqpowT[0:64, 512 * half: 512 * (half + 1)],
                         start=True, stop=False)
                t.matmul(KB[bk][0:1, :], onesf2[0:1, 0:1],
                         vqob_row[0:1, 512 * half: 512 * (half + 1)],
                         start=False, stop=True).inc("t", 1, f"vqc_mm_{b}_{half}")
            v.wait(f"vqc_mm_{b}_0", f"vqc_mm_{b}_1")
            if b == 0:
                v.wait("ld_ratio", "ld_oob", "b2f_row_ok", "ld_a2rep", "ld_gam",
                       "a1s", "G_done_1", "parT_ev")
            v.scalar_tensor_tensor(C_rowsb[0:1, D * b: D * b + 512],
                                   KB[bk0][0:1, :], ratio_rep[0:1, 0:1],
                                   oob_rep2[0:1, 0:512], ALU.mult, ALU.add)
            v.scalar_tensor_tensor(C_rowsb[0:1, D * b + 512: D * (b + 1)],
                                   KB[bk1][0:1, :], ratio_rep[0:1, 0:1],
                                   oob_rep2[0:1, 512:1024], ALU.mult,
                                   ALU.add).inc("v", 1, f"vqcR_ev_{b}")
        v.drain()
        v.tensor_copy(oowT[64:65, :], C_rowsb[0:1, 0:D]).inc("v", 1, "phaseB_done2")

        if debug:
            g.wait("phaseB_done2")
            for nm, t_sb in [("d_rstd", rstd_c), ("d_negm", negm_c),
                             ("d_alpha1", alpha1_c), ("d_b2f", b2f_row),
                             ("d_hms", hm_s), ("d_Gb", G_b), ("d_Crows", C_rowsb),
                             ("d_vqin", vq_in_s), ("d_scores", scores_s),
                             ("d_vqcode", vq_code_s), ("d_hmT", hmT),
                             ("d_b1c", b1_c), ("d_vqcs", vqc_s)]:
                g.dma_start(out=dbg[nm][:, :], in_=t_sb[:, :]).inc("g", 16)
            g.dma_start(out=dbg["d_a2rep"][:, :], in_=alpha2_rep[0:1, :]).inc("g", 16)
            g.dma_start(out=dbg["d_sqc"][:, :], in_=sq_c[:, :]).inc("g", 16)
            g.dma_start(out=dbg["d_parT"][:, :], in_=paramsT[:, :]).inc("g", 16)
            g.dma_start(out=dbg["d_omg"][:, :], in_=omg[:, :]).inc("g", 16)
            g.dma_start(out=dbg["d_rowscr"][:, :], in_=row_scr[:, :]).inc(
                "g", 16, "dbg_done")

        for tt in range(NTILE + 1):
            def _cast(jj, tt=tt):
                gg = 4 * tt + jj
                li = 64 + gg
                v.wait(f"ldxm_{tt}")
                if tt >= 1 or jj >= 2:
                    pt, pj = (tt, jj - 2) if jj >= 2 else (tt - 1, jj + 2)
                    v.wait(f"t_tr_{pt}_{pj}_1")
                if jj == 0 and tt >= 1:
                    v.wait(f"t_h1_{tt-1}_31")
                v.tensor_scalar(xn_tmb[:, D * (gg % 2): D * (gg % 2 + 1)], xsl(li),
                                negm_c[:, gg: gg + 1], rstd_c[:, gg: gg + 1],
                                ALU.add, ALU.mult).inc("v", 1, f"v_xn_{tt}_{jj}")

            if 0 < tt < NTILE:
                _cast(0)
                _cast(1)
            if tt >= 1:
                te = tt - 1
                bt = te // 8
                TRb, H1b, O2b = banks(te)
                for j in range(4):
                    s.wait(f"v_pre_{te}_{j}")
                    s.drain()
                    s.activation(sqscr[:], pre_buf[:, D * j: D * (j + 1)], AF.Square,
                                 accum_out=ms_t[:, j: j + 1]).inc("s", 1, f"s_sq_{te}_{j}")
                s.drain()
                s.activation(rstd2_t[:], ms_t[:], AF.Ln, bias=konst[:, 2:3],
                             scale=konst[:, 3:4])
                s.drain()
                s.activation(rstd2_t[:], rstd2_t[:], AF.Exp, scale=konst[:, 1:2]).inc(
                    "s", 1, f"s_rstd2_{te}")

                if te == 8:
                    v.wait("t_lie_7_3")
                    v.tensor_copy(oowT[64:65, :], C_rowsb[0:1, D: 2 * D]).inc(
                        "v", 1, "c_swap")
                def _A(j):
                    oj = O2b[j]
                    v.wait(f"t_o2_{te}_{j}")
                    if te >= 1:
                        v.wait(f"st_{te-1}_{j}")
                    v.scalar_tensor_tensor(out_part[:, D * j: D * j + 512],
                                           KB[oj[0]][:, :], 1.0,
                                           alpha2_rep[:, 0:512], ALU.mult, ALU.mult)
                    v.scalar_tensor_tensor(out_part[:, D * j + 512: D * (j + 1)],
                                           KB[oj[1]][:, :], 1.0,
                                           alpha2_rep[:, 512:1024], ALU.mult,
                                           ALU.mult).inc("v", 1, f"v_ab_{te}_{j}")

                def _C(j):
                    li = 64 + 4 * te + j
                    v.wait(f"ldxm_{te}")
                    v.tensor_tensor(out_part[:, D * j: D * (j + 1)],
                                    out_part[:, D * j: D * (j + 1)], xsl(li),
                                    ALU.add).inc("v", 1, f"v_ffnx_{te}_{j}")

                def _D(j):
                    li = 64 + 4 * te + j
                    gg = 4 * te + j
                    v.tensor_scalar(pre_buf[:, D * j: D * (j + 1)], xsl(li),
                                    negm_c[:, gg: gg + 1], rstd_c[:, gg: gg + 1],
                                    ALU.add, ALU.mult)

                def _EF(j):
                    oj = O2b[j]
                    v.wait(f"t_lie_{te}_{j}")
                    v.scalar_tensor_tensor(pre_buf[:, D * j: D * j + 512],
                                           KB[oj[0]][:, :], om_rep[:, 0:1],
                                           pre_buf[:, D * j: D * j + 512],
                                           ALU.mult, ALU.add)
                    v.scalar_tensor_tensor(pre_buf[:, D * j + 512: D * (j + 1)],
                                           KB[oj[1]][:, :], om_rep[:, 0:1],
                                           pre_buf[:, D * j + 512: D * (j + 1)],
                                           ALU.mult, ALU.add).inc("v", 1, f"v_pre_{te}_{j}")

                if te == 0:
                    v.wait("phaseB_done2")
                    if debug:
                        v.wait("dbg_done")
                else:
                    v.wait(f"v_out_{te-1}_3")
                _D(0); _A(0); _D(1); _A(1); _EF(0); _D(2); _A(2); _C(0)
                _EF(1); _D(3); _A(3); _C(1); _EF(2); _C(2); _EF(3); _C(3)
                for j in range(4):
                    oj = O2b[j]
                    t.wait(f"v_ab_{te}_{j}")
                    if te == 8 and j == 0:
                        t.wait("c_swap")
                    t.matmul(KB[oj[0]][:, :], x_rot_bf[0:65, 128 * j: 128 * (j + 1)],
                             oowT[0:65, 0:512], start=True, stop=True)
                    t.matmul(KB[oj[1]][:, :], x_rot_bf[0:65, 128 * j: 128 * (j + 1)],
                             oowT[0:65, 512:1024], start=True, stop=True).inc(
                        "t", 1, f"t_lie_{te}_{j}")

                for j in range(4):
                    g.wait(f"v_out_{te}_{j}")
                    g.dma_start(out=out_h[512 * te + 128 * j: 512 * te + 128 * (j + 1), :],
                                in_=out_part[:, D * j: D * (j + 1)]).inc(
                        "st", 16, f"st_{te}_{j}")

            if tt >= NTILE:
                te = tt - 1
                v.wait(f"s_rstd2_{te}", f"v_ffnx_{te}_3")
                for j in range(4):
                    v.scalar_tensor_tensor(pre_buf[:, D * j: D * (j + 1)],
                                           pre_buf[:, D * j: D * (j + 1)],
                                           rstd2_t[:, j: j + 1], gamma_rep[:],
                                           ALU.mult, ALU.mult)
                v.drain()
                for j in range(4):
                    v.tensor_tensor(out_part[:, D * j: D * (j + 1)],
                                    pre_buf[:, D * j: D * (j + 1)],
                                    out_part[:, D * j: D * (j + 1)],
                                    ALU.add).inc("v", 1, f"v_out_{te}_{j}")
                break

            bt = tt // 8
            TRb, H1b, O2b = banks(tt)
            if tt >= 2:
                for j in range(4):
                    sy.wait(f"v_ffnx_{tt-2}_{j}")
            else:
                if tt == 0:
                    sy.wait("w2sgn_7", "om_kc_15")
                for q in range(4):
                    sy.wait(f"xfA_{56 + 4 * tt + q}")
            sy.dma_start(
                out=x_ring[:, D * ((4 * tt) % 8): D * ((4 * tt) % 8) + 4 * D].rearrange(
                    "p (gq d2) -> p gq d2", gq=4),
                in_=x_h[512 * tt: 512 * (tt + 1), :].rearrange(
                    "(gq p) d2 -> p gq d2", p=128)).inc("x", 16, f"ldxm_{tt}")

            for bb in range(16):
                gb = 16 * tt + bb
                if gb >= 3:
                    g.wait(f"s2f_{gb-3}")
                else:
                    g.wait("s2wr_3_1", "opwtr_15_1")
                sl = gb % 3
                g.dma_start(
                    out=s2r[:, 2048 * sl: 2048 * (sl + 1)].rearrange(
                        "p (kk d2) -> p kk d2", kk=2),
                    in_=s2t_scr.rearrange("(kk p) d2 -> p kk d2", p=128)[
                        :, 2 * bb: 2 * bb + 2, :]).inc("s2", 16, f"s2ld_{tt}_{bb}")

            for jj in range(4):
                gg = 4 * tt + jj
                if jj == 1:
                    _cast(2)
                if jj == 2:
                    _cast(3)
                for hf in range(2):
                    v.wait(f"t_tr_{tt}_{jj}_{hf}")
                    dst = xn_fm[:].rearrange("p (c q) -> p c q", c=8)[
                        :, 4 * hf: 4 * hf + 4, 128 * jj: 128 * (jj + 1)]
                    v.tensor_copy(dst, (KB[TRb[0]] if hf == 0 else KB[TRb[1]])[:].rearrange(
                        "p (c q) -> p c q", c=4)).inc("v", 1, f"v_tev_{tt}_{jj}_{hf}")
            v.wait(f"t_xr_{tt}")
            v.tensor_copy(x_rot_bf[0:64, :], KB[TRb[0]][0:64, :]).inc("v", 1, f"v_xrev_{tt}")
            if tt >= 1:
                te = tt - 1
                v.wait(f"s_rstd2_{te}", f"v_ffnx_{te}_3")
                for j in range(4):
                    v.scalar_tensor_tensor(pre_buf[:, D * j: D * (j + 1)],
                                           pre_buf[:, D * j: D * (j + 1)],
                                           rstd2_t[:, j: j + 1], gamma_rep[:],
                                           ALU.mult, ALU.mult)
                v.drain()
                for j in range(4):
                    v.tensor_tensor(out_part[:, D * j: D * (j + 1)],
                                    pre_buf[:, D * j: D * (j + 1)],
                                    out_part[:, D * j: D * (j + 1)],
                                    ALU.add).inc("v", 1, f"v_out_{te}_{j}")

            for jj in range(4):
                gg = 4 * tt + jj
                t.wait(f"v_xn_{tt}_{jj}")
                if jj == 0:
                    if tt == 0:
                        t.wait("G_done_1", "om_ev_1")
                    else:
                        t.wait(f"v_pre_{tt-1}_0")
                if jj >= 1:
                    t.wait(f"v_tev_{tt}_{jj-1}_0", f"v_tev_{tt}_{jj-1}_1")
                for c in range(8):
                    bank = KB[TRb[0]] if c < 4 else KB[TRb[1]]
                    ins = t.matmul(bank[:, 128 * (c % 4): 128 * (c % 4 + 1)],
                                   xn_tmb[:, D * (gg % 2) + 128 * c:
                                          D * (gg % 2) + 128 * (c + 1)],
                                   eyeb[:], start=(c % 4 == 0), stop=(c % 4 == 3))
                    if c in (3, 7):
                        ins.inc("t", 1, f"t_tr_{tt}_{jj}_{c // 4}")
            t.wait(f"v_tev_{tt}_3_0", f"v_tev_{tt}_3_1")
            t.matmul(KB[TRb[0]][0:64, :], G_b[0:64, 64 * bt: 64 * (bt + 1)],
                     xn_fm[0:64, 0:512], start=True, stop=True).inc("t", 1, f"t_xr_{tt}")

            for m in range(32):
                bank = KB[H1b[0]] if (m % 2) == 0 else KB[H1b[1]]
                if m == 0:
                    if tt == 0:
                        t.wait("om_ev_2", "om_ev_3")
                    else:
                        t.wait(f"v_pre_{tt-1}_1")
                if m >= 2:
                    t.wait(f"s_gel_{tt}_{m-2}")
                for c in range(8):
                    ins = t.matmul(bank[:, :],
                                   s1T[:, H * c + 128 * m: H * c + 128 * (m + 1)],
                                   xn_fm[:, 512 * c: 512 * (c + 1)],
                                   start=(c == 0), stop=(c == 7))
                ins.inc("t", 1, f"t_h1_{tt}_{m}")

            for k in range(32):
                gb = 16 * tt + k // 2
                if k % 2 == 0:
                    t.wait(f"s2ld_{tt}_{k//2}")
                t.wait(f"s_gel_{tt}_{k}")
                if k == 0:
                    if tt == 0:
                        t.wait("vqcR_ev_0", "vqcR_ev_1")
                    else:
                        t.wait(f"v_pre_{tt-1}_2", f"v_pre_{tt-1}_3")
                    t.wait(f"v_xrev_{tt}", f"s_gel_{tt}_31")
                ins0 = None
                for j in range(4):
                    oj = O2b[j]
                    for hh in range(2):
                        ins = t.matmul(
                            KB[oj[hh]][:, :],
                            g_all[:, 512 * k + 128 * j: 512 * k + 128 * (j + 1)],
                            s2r[:, 2048 * (gb % 3) + 1024 * (k % 2) + 512 * hh:
                                2048 * (gb % 3) + 1024 * (k % 2) + 512 * (hh + 1)],
                            start=(k == 0), stop=False)
                        if j == 3 and hh == 0:
                            ins0 = ins
                ins0.inc("t", 1, f"t_o2k_{tt}_{k}")
                if k % 2 == 1:
                    ins.inc("s2f", 1, f"s2f_{gb}")
            for j in range(4):
                oj = O2b[j]
                t.matmul(KB[oj[0]][:, :], ones1b[0:1, 0:128], b2f_row[0:1, 0:512],
                         start=False, stop=True)
                t.matmul(KB[oj[1]][:, :], ones1b[0:1, 0:128], b2f_row[0:1, 512:1024],
                         start=False, stop=True).inc("t", 1, f"t_o2_{tt}_{j}")

            if debug and tt == 1 and False:
                g.wait("t_h1_0_31", "s_gel_0_31", "v_xrev_0", "s2ld_0_0")
                g.dma_start(out=dbg["d_xnfm"][:, :], in_=xn_fm[:, :]).inc("g", 16)
                g.dma_start(out=dbg["d_gall"][:, :], in_=g_all[:, :]).inc("g", 16)
                g.dma_start(out=dbg["d_s1T"][:, :], in_=s1T[:, 0:H]).inc("g", 16)
                g.dma_start(out=dbg["d_s2r"][:, :], in_=s2r[:, 0:2048]).inc("g", 16)
                g.dma_start(out=dbg["d_xrot"][:, :], in_=x_rot_bf[:, :]).inc(
                    "g", 16, "dbg2_done")
            for m in range(32):
                s.wait(f"t_h1_{tt}_{m}")
                if tt >= 1:
                    s.wait(f"t_o2k_{tt-1}_{m}")
                if debug and tt == 1 and m == 0 and False:
                    s.wait("dbg2_done")
                bank = KB[H1b[0]] if (m % 2) == 0 else KB[H1b[1]]
                s.activation(g_all[:, 512 * m: 512 * (m + 1)], bank[:, :], AF.Gelu,
                             bias=b1_c[:, m: m + 1], scale=alpha1_c[:, m: m + 1]).inc(
                    "s", 1, f"s_gel_{tt}_{m}")

        g.wait(f"st_{NTILE-1}_0", f"st_{NTILE-1}_1", f"st_{NTILE-1}_2",
               f"st_{NTILE-1}_3")

    E.reset_pass()
    program(E)
    missing = E.waited - set(E.events)
    assert not missing, f"waits on undefined events: {sorted(missing)[:10]}"
    semv = {}
    ptr = {k: 0 for k in E.streams}
    progress = True
    while progress:
        progress = False
        for k, st_ in E.streams.items():
            while ptr[k] < len(st_):
                it = st_[ptr[k]]
                if it[0] == "wait":
                    sem, val = E.events[it[1]]
                    if semv.get(sem, 0) < val:
                        break
                elif it[0] == "waitv":
                    if semv.get(it[1], 0) < it[2]:
                        break
                else:
                    semv[it[2] if False else it[1]] = semv.get(it[1], 0) + it[2]
                ptr[k] += 1
                progress = True
    stuck = {k: (ptr[k], len(E.streams[k]),
                 E.streams[k][ptr[k]] if ptr[k] < len(E.streams[k]) else None)
             for k in E.streams if ptr[k] < len(E.streams[k])}
    assert not stuck, f"DEADLOCK: {stuck}"
    E.counting = False
    with nc.Block() as block:
        @block.tensor
        def _(eng):
            E.reset_pass()
            E.cur, E.handle = "t", eng
            program(E)

        @block.vector
        def _(eng):
            E.reset_pass()
            E.cur, E.handle = "v", eng
            program(E)

        @block.scalar
        def _(eng):
            E.reset_pass()
            E.cur, E.handle = "s", eng
            program(E)

        @block.sync
        def _(eng):
            E.reset_pass()
            E.cur, E.handle = "sy", eng
            program(E)

        @block.gpsimd
        def _(eng):
            E.reset_pass()
            E.cur, E.handle = "g", eng
            program(E)

    return nc, ctx


_TRIU_R, _TRIU_C = np.triu_indices(OD, k=1)


def _mconst():
    m = np.zeros((NP, OD * OD), dtype=ml_dtypes.bfloat16)
    r = np.arange(NP)
    m[r, OD * _TRIU_R + _TRIU_C] = 1.0
    m[r, OD * _TRIU_C + _TRIU_R] = -1.0
    return m


def kernel(**inputs):
    def f32(key):
        return np.ascontiguousarray(np.asarray(inputs[key], dtype=np.float32))

    x = f32("L0_attn_x").reshape(B * L, D)
    shared = {
        "w1": f32("L0_ffn_w1"),
        "w2": f32("L0_ffn_w2"),
        "b1": f32("L0_ffn_b1").reshape(1, H),
        "b2": f32("L0_ffn_b2").reshape(1, D),
        "opw": f32("L0_attn_omega_proj_w"),
        "opb": f32("L0_attn_omega_proj_b").reshape(1, NP),
        "oow": f32("L0_attn_omega_out_w"),
        "oob": f32("L0_attn_omega_out_b").reshape(1, D),
        "vqw": f32("L0_attn_vq_proj_in_w"),
        "vqb": f32("L0_attn_vq_proj_in_b").reshape(1, VQD),
        "vqow": f32("L0_attn_vq_proj_out_w"),
        "vqob": f32("L0_attn_vq_proj_out_b").reshape(1, D),
        "cb": f32("L0_attn_codebook"),
        "omix": f32("L0_attn_omega_mix").reshape(1, 1),
        "vmix": f32("L0_attn_vq_mix").reshape(1, 1),
        "gam": f32("L0_attn_norm_w").reshape(1, D),
        "mconst": _mconst(),
        "eyef": np.eye(128, dtype=np.float32),
        "eyeb": np.eye(128, dtype=ml_dtypes.bfloat16),
    }
    in_maps = []
    for c in range(NCORE):
        m = dict(shared)
        m["x"] = np.ascontiguousarray(x[TCORE * c: TCORE * (c + 1), :])
        in_maps.append(m)

    nc, ctx = build_program()
    res = run_bass_kernel_spmd(nc, in_maps, core_ids=list(range(NCORE)))
    out = np.concatenate([np.asarray(r["out"]) for r in res.results], axis=0)
    return out.reshape(B, L, D).astype(np.float32)


# version 27 (final)
# speedup vs baseline: 1.0836x; 1.0836x over previous
import dataclasses
import numpy as np
import ml_dtypes
import concourse.bass as bass
import concourse.mybir as mybir
from concourse.bass_utils import run_bass_kernel_spmd
from contextlib import ExitStack

F32 = mybir.dt.float32
BF16 = mybir.dt.bfloat16
ALU = mybir.AluOpType
AF = mybir.ActivationFunctionType
AX = mybir.AxisListType

B, L, D, H = 16, 4096, 1024, 4096
NCORE = 8
NB = B // NCORE
TCORE = NB * L
NGRP = TCORE // 128
NTILE = TCORE // 512
OD, VQN, VQD = 64, 256, 64
NP = OD * (OD - 1) // 2
NPK = 126
LN_EPS = 1e-5
RMS_EPS = 1e-6


def bc(ap, parts, free):
    return dataclasses.replace(ap, ap=[[0, parts]] + list(free))


class Emitter:

    def __init__(self):
        self.counting = True
        self.cur = None
        self.handle = None
        self.ctr = {}
        self.events = {}
        self.waited = set()
        self.sems = {}
        self.dma_i = {"g": 0, "sy": 0}
        self.streams = {k: [] for k in ("t", "v", "s", "sy", "g")}
        self.last_inst_eng = None

    def reset_pass(self):
        self.dma_i = {"g": 0, "sy": 0}

    class _Inst:
        __slots__ = ("em", "inst", "dma_ctx")

        def __init__(self, em, inst, dma_ctx=None):
            self.em, self.inst, self.dma_ctx = em, inst, dma_ctx

        def inc(self, sem, n=1, ev=None):
            em = self.em
            if self.dma_ctx is not None:
                sem, val = self.dma_ctx
                if em.counting:
                    if ev is not None:
                        assert ev not in em.events, f"dup event {ev}"
                        em.events[ev] = (sem, val)
                    em.streams[em.last_inst_eng].append(("inc", sem, 16, ev))
                elif self.inst is not None:
                    self.inst.then_inc(em.sems[sem], 16)
                return self
            if em.counting:
                em.ctr[sem] = em.ctr.get(sem, 0) + n
                if ev is not None:
                    assert ev not in em.events, f"dup event {ev}"
                    em.events[ev] = (sem, em.ctr[sem])
                em.streams[em.last_inst_eng].append(("inc", sem, n, ev))
            elif self.inst is not None:
                self.inst.then_inc(em.sems[sem], n)
            return self

    class _Eng:
        __slots__ = ("_em", "_key")

        def __init__(self, em, key):
            self._em, self._key = em, key

        def wait(self, *evs):
            em = self._em
            for ev in evs:
                if em.counting:
                    em.waited.add(ev)
                    em.streams[self._key].append(("wait", ev))
                elif em.cur == self._key:
                    sem, val = em.events[ev]
                    em.handle.wait_ge(em.sems[sem], val)

        def __getattr__(self, op):
            em, key = self._em, self._key

            def f(*a, **kw):
                if op == "dma_start" and key in ("g", "sy"):
                    i = em.dma_i[key]
                    em.dma_i[key] = i + 1
                    if key == "g":
                        sem, val = f"g{i % 4}", 16 * (i // 4 + 1)
                        if em.counting and i >= 4:
                            em.streams[key].append(("waitv", f"g{i % 4}", 16 * (i // 4)))
                        if (not em.counting) and em.cur == key and i >= 4:
                            em.handle.wait_ge(em.sems[sem], 16 * (i // 4))
                    else:
                        sem, val = f"x{i % 8}", 16 * (i // 8 + 1)
                    if em.counting:
                        em.last_inst_eng = key
                        return Emitter._Inst(em, None, (sem, val))
                    if em.cur != key:
                        return Emitter._Inst(em, None, (sem, val))
                    return Emitter._Inst(em, getattr(em.handle, op)(*a, **kw),
                                         (sem, val))
                if em.counting:
                    em.last_inst_eng = key
                    return Emitter._Inst(em, None)
                if em.cur != key:
                    return Emitter._Inst(em, None)
                return Emitter._Inst(em, getattr(em.handle, op)(*a, **kw))

            return f

    def eng(self, key):
        return Emitter._Eng(self, key)


def build_program(debug=False):
    nc = bass.Bass("TRN2", target_bir_lowering=False)
    dt = nc.dram_tensor
    x_h = dt("x", [TCORE, D], F32, kind="ExternalInput").ap()
    out_h = dt("out", [TCORE, D], F32, kind="ExternalOutput").ap()
    w1_h = dt("w1", [H, D], F32, kind="ExternalInput").ap()
    w2_h = dt("w2", [D, H], F32, kind="ExternalInput").ap()
    b1_h = dt("b1", [1, H], F32, kind="ExternalInput").ap()
    b2_h = dt("b2", [1, D], F32, kind="ExternalInput").ap()
    opw_h = dt("opw", [NP, D], F32, kind="ExternalInput").ap()
    opb_h = dt("opb", [1, NP], F32, kind="ExternalInput").ap()
    oow_h = dt("oow", [D, OD], F32, kind="ExternalInput").ap()
    oob_h = dt("oob", [1, D], F32, kind="ExternalInput").ap()
    vqw_h = dt("vqw", [VQD, D], F32, kind="ExternalInput").ap()
    vqb_h = dt("vqb", [1, VQD], F32, kind="ExternalInput").ap()
    vqow_h = dt("vqow", [D, VQD], F32, kind="ExternalInput").ap()
    vqob_h = dt("vqob", [1, D], F32, kind="ExternalInput").ap()
    cb_h = dt("cb", [VQN, VQD], F32, kind="ExternalInput").ap()
    omix_h = dt("omix", [1, 1], F32, kind="ExternalInput").ap()
    vmix_h = dt("vmix", [1, 1], F32, kind="ExternalInput").ap()
    gam_h = dt("gam", [1, D], F32, kind="ExternalInput").ap()
    mconst_h = dt("mconst", [NP, OD * OD], BF16, kind="ExternalInput").ap()
    eyef_h = dt("eyef", [128, 128], F32, kind="ExternalInput").ap()
    eyeb_h = dt("eyeb", [128, 128], BF16, kind="ExternalInput").ap()

    s2t_scr = dt("s2t_scr", [H, D], BF16, kind="Internal").ap()
    row_scr = dt("row_scr", [16, D], F32, kind="Internal").ap()
    dbg = {}
    if debug:
        for nm, shp, dtp in [
            ("d_rstd", [128, NGRP], F32), ("d_negm", [128, NGRP], F32),
            ("d_alpha1", [128, 32], F32), ("d_a2rep", [1, D], F32),
            ("d_b2f", [1, D], BF16), ("d_hms", [2, D], F32),
            ("d_Gb", [64, 2 * OD], BF16), ("d_Crows", [1, 2 * D], BF16),
            ("d_vqin", [2, VQD], F32), ("d_scores", [2, VQN], F32),
            ("d_vqcode", [2, VQD], F32), ("d_hmT", [128, 16], F32),
            ("d_b1c", [128, 32], F32), ("d_vqcs", [2, D], F32),
            ("d_xnfm", [128, 8 * 512], BF16), ("d_gall", [128, 32 * 512], BF16),
            ("d_s1T", [128, H], BF16), ("d_s2r", [128, 2048], BF16),
            ("d_xrot", [64, 512], BF16),
            ("d_rowscr", [16, D], F32), ("d_sqc", [128, NGRP], F32),
            ("d_parT", [NPK, 32], BF16), ("d_omg", [64, 64], F32),
        ]:
            dbg[nm] = dt(nm, shp, dtp, kind="ExternalOutput").ap()

    ctx = ExitStack()

    def sb(name, shape, dtp):
        return ctx.enter_context(nc.sbuf_tensor(name, shape, dtp))

    s1T = sb("s1T", [128, 8 * H], BF16)
    g_all = sb("g_all", [128, 32 * 512], BF16)
    xn_fm = sb("xn_fm", [128, 8 * 512], BF16)
    xn_tmb = sb("xn_tmb", [128, 2 * D], BF16)
    x_ring = sb("x_ring", [128, 8 * D], F32)
    s2r = sb("s2r", [128, 3 * 2048], BF16)
    pre_buf = sb("pre_buf", [128, 4 * D], F32)
    out_part = sb("out_part", [128, 4 * D], F32)
    sqscr = sb("sqscr", [128, D], F32)
    x_rot_bf = sb("x_rot_bf", [65, 512], BF16)
    oowT = sb("oowT", [65, D], BF16)
    gamma_rep = sb("gamma_rep", [128, D], F32)
    alpha2_rep = sb("alpha2_rep", [128, D], F32)
    b2f_row = sb("b2f_row", [1, D], BF16)
    C_rowsb = sb("C_rowsb", [1, 2 * D], BF16)
    alpha1_c = sb("alpha1_c", [128, 32], F32)
    b1_c = sb("b1_c", [128, 32], F32)
    negm_c = sb("negm_c", [128, NGRP], F32)
    rstd_c = sb("rstd_c", [128, NGRP], F32)
    sq_c = sb("sq_c", [128, NGRP], F32)
    m2t = sb("m2t", [128, 1], F32)
    ms_t = sb("ms_t", [128, 4], F32)
    rstd2_t = sb("rstd2_t", [128, 4], F32)
    pa_b = sb("pa_b", [128, 8], BF16)
    onescol_b = sb("onescol_b", [128, 1], BF16)
    eyef = sb("eyef_s", [128, 128], F32)
    eyeb = sb("eyeb_s", [128, 128], BF16)
    ones1b = sb("ones1b", [1, 128], BF16)
    onesf2 = sb("onesf2", [1, 2], F32)
    konst = sb("konst", [128, 8], F32)
    alpha2c = sb("alpha2c", [128, 8], F32)
    b2c = sb("b2c", [128, 8], F32)
    b2fc = sb("b2fc", [128, 8], F32)
    om_rep = sb("om_rep", [128, 1], F32)
    vm_rep = sb("vm_rep", [128, 1], F32)
    ratio_rep = sb("ratio_rep", [128, 1], F32)
    rr_t = sb("rr_t", [1, 1], F32)
    hmT = sb("hmT", [128, 16], F32)
    hmTb = sb("hmTb", [128, 16], BF16)
    G_b = sb("G_b", [64, 2 * OD], BF16)
    paramsT = sb("paramsT", [NPK, 32], BF16)

    gaddr = nc.lookup_mloc(g_all).addr

    def ovA(name, shape, dtp, off):
        return nc.alloc_sbuf_tensor_at(name, shape, dtp, offset=gaddr + off)

    w1buf = ovA("w1buf", [128, 4 * D], F32, 0)
    w1sgn = ovA("w1sgn", [128, 4 * D], BF16, 16384)
    w2buf_a = ovA("w2buf_a", [128, H], F32, 0)
    w2buf_b = ovA("w2buf_b", [128, H], F32, 16384)
    opwT_full = ovA("opwT_full", [128, 8 * NP], BF16, 0)
    mbuf = ovA("mbuf", [NPK, 4 * OD * OD], BF16, 0)
    om_stage = ovA("om_stage", [2, OD * OD], F32, 0)

    paddr = nc.lookup_mloc(pre_buf).addr
    oaddr = nc.lookup_mloc(out_part).addr

    def ovP(name, shape, dtp, off):
        return nc.alloc_sbuf_tensor_at(name, shape, dtp, offset=paddr + off)

    def ovO(name, shape, dtp, off):
        return nc.alloc_sbuf_tensor_at(name, shape, dtp, offset=oaddr + off)

    w2sgnB = ovP("w2sgnB", [128, 2 * H], BF16, 0)
    crow_stage = ovP("crow_stage", [1, 2 * D], F32, 0)
    params_s = ovP("params_s", [2, NP], F32, 0)
    vqc_s = ovP("vqc_s", [2, D], F32, 0)
    oob_rep2 = ovP("oob_rep2", [2, D], F32, 4096)
    vqwT = ovP("vqwT", [128, 8 * VQD], F32, 8192)
    scores_s = ovP("scores_s", [2, VQN], F32, 8192)
    onehot = ovP("onehot", [2, VQN], F32, 9216)
    ncbT = ovP("ncbT", [64, VQN], F32, 10240)
    cb_s = ovP("cb_s", [128, 2 * VQD], F32, 11264)
    ncb = ovP("ncb", [128, 2 * VQD], F32, 11776)
    cbsq = ovP("cbsq", [128, VQD], F32, 12288)
    vq_in_s = ovP("vq_in_s", [2, VQD], F32, 12544)
    nvq = ovP("nvq", [2, VQD], F32, 12800)
    nvqT = ovP("nvqT", [64, 2], F32, 13056)
    vq_code_s = ovP("vq_code_s", [2, VQD], F32, 13088)
    vq_codeT = ovP("vq_codeT", [64, 2], F32, 13344)
    normt = ovP("normt", [128, 2], F32, 13376)
    vnorm = ovP("vnorm", [2, 1], F32, 13408)
    onehotT = ovP("onehotT", [128, 4], F32, 13440)
    omg = ovP("omg", [64, 64], F32, 13472)
    omg2 = ovP("omg2", [64, 64], F32, 13728)
    omg3 = ovP("omg3", [64, 64], F32, 13984)
    gt1 = ovP("gt1", [64, 64], F32, 14240)
    vqb_row = ovP("vqb_row", [1, VQD], F32, 14496)

    s2stage = ovO("s2stage", [128, 2 * 4096], BF16, 0)
    vqob_row = ovO("vqob_row", [1, D], F32, 0)
    opb_b = ovO("opb_b", [1, NP], BF16, 4096)
    hm_s = ovO("hm_s", [2, D], F32, 8192)
    vqpowT = ovO("vqpowT", [64, D], F32, 12288)

    smallbuf = nc.alloc_sbuf_tensor_at("smallbuf", [128, 2 * D], F32,
                                       offset=nc.lookup_mloc(s2r).addr)

    KB = [ctx.enter_context(nc.psum_tensor(f"K{i}", [128, 512], F32)) for i in range(8)]

    sems = {n: ctx.enter_context(nc.semaphore(f"sm_{n}"))
            for n in (["t", "v", "s", "s2f"]
                      + [f"g{i}" for i in range(4)]
                      + [f"x{i}" for i in range(8)])}

    E = Emitter()
    E.sems = sems

    def xsl(li):
        sl = li % 8
        return x_ring[:, D * sl: D * (sl + 1)]

    def banks(tt):
        if tt % 2 == 0:
            return (0, 1), (2, 3), [(4, 5), (6, 7), (0, 1), (2, 3)]
        return (4, 5), (6, 7), [(0, 1), (2, 3), (4, 5), (6, 7)]

    def program(E):
        t = E.eng("t"); v = E.eng("v"); s = E.eng("s"); sy = E.eng("sy"); g = E.eng("g")

        g.dma_start(out=eyef[:], in_=eyef_h[:, :]).inc("g", 16, "ld_eyef")
        g.dma_start(out=eyeb[:], in_=eyeb_h[:, :]).inc("g", 16, "ld_eyeb")
        pass
        g.dma_start(out=om_rep[:], in_=bc(omix_h, 128, [[1, 1]])).inc("g", 16, "ld_om")
        g.dma_start(out=vm_rep[:], in_=bc(vmix_h, 128, [[1, 1]])).inc("g", 16, "ld_vm")
        v.wait("ld_om", "ld_vm")
        v.reciprocal(rr_t[:, :], om_rep[0:1, 0:1])
        v.drain()
        v.tensor_tensor(rr_t[:, :], rr_t[:, :], vm_rep[0:1, 0:1], ALU.mult).inc(
            "v", 1, "ratio_ok")
        g.wait("ratio_ok")
        g.dma_start(out=row_scr[15:16, 0:1], in_=rr_t[:, :]).inc("g", 16, "rt_ratio")
        g.wait("rt_ratio")
        g.dma_start(out=ratio_rep[:], in_=bc(row_scr[15:16, 0:1], 128, [[1, 1]])).inc(
            "g", 16, "ld_ratio")
        g.dma_start(out=smallbuf[0:32, 0:128],
                    in_=b1_h.rearrange("o (r p) -> (o r) p", p=128)).inc("g", 16, "ld_b1")
        g.dma_start(out=smallbuf[0:8, 128:256],
                    in_=b2_h.rearrange("o (r p) -> (o r) p", p=128)).inc("g", 16, "ld_b2")

        t.wait("ld_b1", "ld_eyef")
        t.matmul(KB[2][0:128, 0:32], smallbuf[0:32, 0:128], eyef[0:32, 0:32],
                 start=True, stop=True).inc("t", 1, "b1_tr")
        t.wait("ld_b2")
        t.matmul(KB[3][0:128, 0:8], smallbuf[0:8, 128:256], eyef[0:8, 0:8],
                 start=True, stop=True).inc("t", 1, "b2_tr")
        v.wait("b1_tr")
        v.tensor_copy(b1_c[:], KB[2][0:128, 0:32]).inc("v", 1, "b1c_ev")
        v.wait("b2_tr")
        v.tensor_copy(b2c[:], KB[3][0:128, 0:8]).inc("v", 1, "b2c_ev")
        v.wait("ld_eyeb")
        v.memset(ones1b[:], 1.0)
        v.memset(onesf2[:], 1.0)
        v.memset(konst[:, 0:1], LN_EPS)
        v.memset(konst[:, 1:2], -0.5)
        v.memset(konst[:, 2:3], RMS_EPS)
        v.memset(konst[:, 3:4], 1.0 / D)
        v.memset(konst[:, 4:5], 0.5)
        v.memset(x_rot_bf[64:65, :], 1.0)
        v.memset(onescol_b[:], 1.0).inc("v", 1, "consts_v")

        def _pa_front(rr):
            if rr >= 2:
                for q in range(4):
                    sy.wait(f"xfA_{4*(rr-2)+q}")
            sy.dma_start(
                out=x_ring[:, D * ((4 * rr) % 8): D * ((4 * rr) % 8) + 4 * D].rearrange(
                    "p (gq d2) -> p gq d2", gq=4),
                in_=x_h[512 * rr: 512 * (rr + 1), :].rearrange(
                    "(gq p) d2 -> p gq d2", p=128)).inc("x", 16, f"ldx4_{rr}")
            for q in range(4):
                gg = 4 * rr + q
                s.wait(f"ldx4_{rr}")
                if gg == 0:
                    s.wait("consts_v")
                s.drain()
                s.activation(sqscr[:], xsl(gg), AF.Copy,
                             accum_out=negm_c[:, gg: gg + 1])
                s.drain()
                ins = s.activation(sqscr[:], xsl(gg), AF.Square,
                                   accum_out=sq_c[:, gg: gg + 1])
                if q == 3:
                    ins.inc("s", 1, f"pa_sq_r{rr}")

        def _pa_back(rr):
            sl = slice(4 * rr, 4 * rr + 4)
            v.wait(f"pa_sq_r{rr}")
            v.tensor_scalar(negm_c[:, sl], negm_c[:, sl], -1.0 / D, None, ALU.mult)
            v.drain()
            v.tensor_tensor(rstd_c[:, sl], negm_c[:, sl], negm_c[:, sl], ALU.mult)
            v.drain()
            v.scalar_tensor_tensor(sq_c[:, sl], sq_c[:, sl], 1.0 / D,
                                   rstd_c[:, sl], ALU.mult, ALU.subtract).inc(
                "v", 1, f"pa_var_r{rr}")
            s.wait(f"pa_var_r{rr}")
            s.activation(rstd_c[:, sl], sq_c[:, sl], AF.Ln, bias=konst[:, 0:1])
            s.drain()
            s.activation(rstd_c[:, sl], rstd_c[:, sl], AF.Exp,
                         scale=konst[:, 1:2]).inc("s", 1, f"pa_rstd_r{rr}")
            for q in range(4):
                gg = 4 * rr + q
                if gg >= 4:
                    v.wait(f"xfA_{gg-4}")
                ins = v.tensor_copy(xn_fm[:, D * (gg % 4): D * (gg % 4 + 1)], xsl(gg))
                if q == 3:
                    ins.inc("v", 1, f"pa_xbr_{rr}")
            v.wait(f"pa_rstd_r{rr}")
            v.tensor_tensor(sq_c[:, sl], negm_c[:, sl], rstd_c[:, sl], ALU.mult)
            v.drain()
            v.tensor_copy(pa_b[:, 0:4], rstd_c[:, sl])
            v.tensor_copy(pa_b[:, 4:8], sq_c[:, sl]).inc("v", 1, f"pa_nmr_r{rr}")
            t.wait(f"pa_nmr_r{rr}", f"pa_xbr_{rr}")
            if rr == 0:
                t.wait("b1c_ev", "b2c_ev", "consts_v")
            for q in range(4):
                gg = 4 * rr + q
                bt = gg // 32
                st0 = (gg % 32) == 0
                sp0 = (gg % 32) == 31
                xb = xn_fm[:, D * (gg % 4): D * (gg % 4 + 1)]
                t.matmul(KB[4 + 2 * bt][0:1, :], pa_b[:, q: q + 1],
                         xb[:, 0:512], start=st0, stop=sp0)
                t.matmul(KB[5 + 2 * bt][0:1, :], pa_b[:, q: q + 1],
                         xb[:, 512:1024], start=st0, stop=sp0)
                t.matmul(KB[2 + bt][0:1, 0:1], pa_b[:, 4 + q: 5 + q], onescol_b[:],
                         start=st0, stop=sp0).inc("t", 1, f"xfA_{gg}")



        for i in range(32):
            if i % 2 == 1:
                _pa_front((i - 1) // 2)
                _pa_back((i - 1) // 2)
            if i >= 4:
                g.wait(f"w1sgn_{i-4}", f"w1abs_{i-4}")
            if i == 0:
                g.wait("b1_tr", "b2_tr")
            g.dma_start(out=w1buf[:, D * (i % 4): D * (i % 4 + 1)],
                        in_=w1_h[128 * i: 128 * (i + 1), :]).inc("g", 16, f"ld_w1_{i}")
            s.wait(f"ld_w1_{i}")
            s.sign(w1sgn[:, D * (i % 4): D * (i % 4 + 1)],
                   w1buf[:, D * (i % 4): D * (i % 4 + 1)]).inc("s", 1, f"w1sgn_{i}")
            v.wait(f"ld_w1_{i}")
            v.reduce_sum(alpha1_c[:, i: i + 1], w1buf[:, D * (i % 4): D * (i % 4 + 1)],
                         axis=AX.X, apply_absolute_value=True).inc("v", 1, f"w1abs_{i}")
            t.wait(f"w1sgn_{i}")
            if i == 0:
                t.wait("ld_eyeb")
            if i >= 1:
                t.wait(f"s1Tev_{i-1}_1")
            for c in range(8):
                bank = KB[0] if c < 4 else KB[1]
                ins = t.matmul(bank[:, 128 * (c % 4): 128 * (c % 4 + 1)],
                               w1sgn[:, D * (i % 4) + 128 * c: D * (i % 4) + 128 * (c + 1)],
                               eyeb[:], start=(c % 4 == 0), stop=(c % 4 == 3))
                if c in (3, 7):
                    ins.inc("t", 1, f"w1tr_{i}_{c // 4}")
            for hf in range(2):
                v.wait(f"w1tr_{i}_{hf}")
                dst = s1T[:].rearrange("p (c m) -> p c m", c=8)[
                    :, 4 * hf: 4 * hf + 4, 128 * i: 128 * (i + 1)]
                v.tensor_copy(dst, (KB[0] if hf == 0 else KB[1])[:].rearrange(
                    "p (c m) -> p c m", c=4)).inc("v", 1, f"s1Tev_{i}_{hf}")

        for bt in range(2):
            v.wait(f"xfA_{32*bt+31}")
            if bt == 0:
                v.wait("rt_ratio")
            if bt == 1:
                v.wait("hm_w_0")
            v.tensor_copy(rr_t[0:1, 0:1], KB[2 + bt][0:1, 0:1])
            v.drain()
            v.tensor_scalar(hm_s[0:1, 0:512], KB[4 + 2 * bt][0:1, :],
                            rr_t[0:1, 0:1], 1.0 / L, ALU.add, ALU.mult)
            v.tensor_scalar(hm_s[0:1, 512:1024], KB[5 + 2 * bt][0:1, :],
                            rr_t[0:1, 0:1], 1.0 / L, ALU.add, ALU.mult).inc(
                "v", 1, f"hm_row_{bt}")
            g.wait(f"hm_row_{bt}")
            g.dma_start(out=row_scr[2 + bt: 3 + bt, :], in_=hm_s[0:1, :]).inc(
                "g", 16, f"hm_w_{bt}")
        for pair in range(4):
            for hf in range(2):
                i = 2 * pair + hf
                wb = w2buf_a if hf == 0 else w2buf_b
                if pair >= 1:
                    sy.wait(f"w2sgn_{i-2}", f"w2abs_{i-2}")
                else:
                    sy.wait("w1sgn_31", "w1abs_31", "w1tr_31_1")
                sy.dma_start(out=wb[:], in_=w2_h[128 * i: 128 * (i + 1), :]).inc(
                    "x", 16, f"ld_w2_{i}")
                v.wait(f"ld_w2_{i}")
                v.reduce_sum(alpha2c[:, i: i + 1], wb[:], axis=AX.X,
                             apply_absolute_value=True).inc("v", 1, f"w2abs_{i}")
            s.wait(f"ld_w2_{2*pair}")
            if pair >= 1:
                s.wait(f"w2tr_{pair-1}_15")
            s.sign(w2sgnB[:, 0:2048], w2buf_a[:, 0:2048]).inc(
                "s", 1, f"w2q_{pair}_0")
            s.wait(f"ld_w2_{2*pair+1}")
            s.sign(w2sgnB[:, H: H + 2048], w2buf_b[:, 0:2048]).inc(
                "s", 1, f"w2q_{pair}_2")
            if pair >= 1:
                s.wait(f"w2tr_{pair-1}_31")
            s.sign(w2sgnB[:, 2048: 4096], w2buf_a[:, 2048:4096]).inc(
                "s", 1, f"w2sgn_{2*pair}")
            s.sign(w2sgnB[:, H + 2048: H + 4096], w2buf_b[:, 2048:4096]).inc(
                "s", 1, f"w2sgn_{2*pair+1}")
            t.wait(f"w2q_{pair}_0", f"w2q_{pair}_2")
            if pair == 0:
                t.wait("s1Tev_31_1", "hm_row_1")
            for m in range(32):
                bank = KB[m % 4]
                if m == 16:
                    t.wait(f"w2sgn_{2*pair}", f"w2sgn_{2*pair+1}")
                if pair > 0 or m >= 4:
                    pp, pm = (pair, m - 4) if m >= 4 else (pair - 1, 28 + m)
                    t.wait(f"s2ev_{pp}_{pm}")
                t.matmul(bank[:, 0:128], w2sgnB[:, 128 * m: 128 * (m + 1)],
                         eyeb[:], start=True, stop=False)
                t.matmul(bank[:, 128:256], w2sgnB[:, H + 128 * m: H + 128 * (m + 1)],
                         eyeb[:], start=False, stop=True).inc("t", 1, f"w2tr_{pair}_{m}")
                v.wait(f"w2tr_{pair}_{m}")
                slot = m // 16
                if pair == 0 and m == 16:
                    v.wait("hm_w_1")
                if pair > 0 or slot > 0:
                    ps_, pslot = (pair, slot - 1) if slot > 0 else (pair - 1, 1)
                    v.wait(f"s2wr_{ps_}_{pslot}")
                v.tensor_copy(
                    s2stage[:, 4096 * slot + 256 * (m % 16): 4096 * slot + 256 * (m % 16) + 256],
                    KB[m % 4][:, 0:256]).inc("v", 1, f"s2ev_{pair}_{m}")
                if m % 16 == 15:
                    g.wait(f"s2ev_{pair}_{m}")
                    dst = s2t_scr.rearrange("(mm p) d2 -> p mm d2", p=128)[
                        :, 16 * slot: 16 * (slot + 1), 256 * pair: 256 * (pair + 1)]
                    src = s2stage[:, 4096 * slot: 4096 * (slot + 1)].rearrange(
                        "p (mm d2) -> p mm d2", mm=16)
                    g.dma_start(out=dst, in_=src).inc("g", 16, f"s2wr_{pair}_{slot}")

        v.wait("w2abs_7")
        v.tensor_scalar(alpha2c[:], alpha2c[:], 1.0 / H, None, ALU.mult).inc("v", 1, "a2s")
        v.wait("b2c_ev", "a2s")
        v.reciprocal(b2fc[:], alpha2c[:])
        v.drain()
        v.tensor_tensor(b2fc[:], b2fc[:], b2c[:], ALU.mult).inc("v", 1, "b2f")
        v.wait("w1abs_31")
        v.tensor_scalar(alpha1_c[:], alpha1_c[:], 1.0 / D, None, ALU.mult).inc("v", 1, "a1s")
        t.wait("a2s", "b2f", "s2ev_3_31")
        t.matmul(KB[0][0:8, 0:128], alpha2c[:, :], eyef[:, :],
                 start=True, stop=True).inc("t", 1, "a2T_tr")
        t.matmul(KB[1][0:8, 0:128], b2fc[:, :], eyef[:, :],
                 start=True, stop=True).inc("t", 1, "b2fT_tr")
        v.wait("a2T_tr")
        v.tensor_copy(gamma_rep[0:8, 0:128], KB[0][0:8, 0:128]).inc("v", 1, "a2T_ev")
        v.wait("b2fT_tr")
        v.tensor_copy(gamma_rep[0:8, 128:256], KB[1][0:8, 0:128]).inc("v", 1, "b2fT_ev")
        g.wait("a2T_ev")
        g.dma_start(out=row_scr.rearrange("(o r) (q p) -> (o q) r p", r=1, p=128)[
            0:8, 0, :] if False else
            row_scr[0:1, :].rearrange("o (r p) -> (o r) p", p=128),
            in_=gamma_rep[0:8, 0:128]).inc("g", 16, "rt_a2w")
        g.wait("b2fT_ev")
        g.dma_start(out=row_scr[1:2, :].rearrange("o (r p) -> (o r) p", p=128),
                    in_=gamma_rep[0:8, 128:256]).inc("g", 16, "rt_b2w")
        g.wait("rt_a2w", "rt_b2w")
        g.dma_start(out=alpha2_rep[:], in_=bc(row_scr[0:1, :], 128, [[1, D]])).inc(
            "g", 16, "ld_a2rep")
        g.dma_start(out=gamma_rep[:], in_=bc(gam_h, 128, [[1, D]])).inc("g", 16, "ld_gam")
        g.wait("rt_b2w", "s2wr_3_1")
        g.dma_start(out=hm_s[0:1, 0:D], in_=row_scr[1:2, :]).inc("g", 16, "ld_b2fs")
        v.wait("ld_b2fs")
        v.tensor_copy(b2f_row[:], hm_s[0:1, 0:D]).inc("v", 1, "b2f_row_ok")

        for i in range(8):
            g.dma_start(out=smallbuf[:, 64 * i: 64 * (i + 1)],
                        in_=oow_h[128 * i: 128 * (i + 1), :]).inc("g", 16, f"ld_oow_{i}")
            g.dma_start(out=smallbuf[:, 512 + 64 * i: 512 + 64 * (i + 1)],
                        in_=vqow_h[128 * i: 128 * (i + 1), :]).inc("g", 16, f"ld_vqow_{i}")
        for i in range(8):
            t.wait(f"ld_oow_{i}")
            if i == 0:
                t.wait("w2tr_3_31", "ld_eyef", "a2T_ev", "b2fT_ev")
            if i > 1:
                t.wait(f"oowev_{i-2}")
            t.matmul(KB[i % 2][0:64, 0:128], smallbuf[:, 64 * i: 64 * (i + 1)],
                     eyef[:], start=True, stop=True).inc("t", 1, f"oowtr_{i}")
            v.wait(f"oowtr_{i}")
            v.tensor_copy(oowT[0:64, 128 * i: 128 * (i + 1)], KB[i % 2][0:64, 0:128]).inc(
                "v", 1, f"oowev_{i}")
        for i in range(8):
            t.wait(f"ld_vqow_{i}")
            if i == 0:
                t.wait("oowev_6", "oowev_7")
            if i > 1:
                t.wait(f"vqowev_{i-2}")
            t.matmul(KB[i % 2][0:64, 0:128], smallbuf[:, 512 + 64 * i: 512 + 64 * (i + 1)],
                     eyef[:], start=True, stop=True).inc("t", 1, f"vqowtr_{i}")
            v.wait(f"vqowtr_{i}")
            if i == 0:
                v.wait("s2ev_3_31")
            v.tensor_copy(vqpowT[0:64, 128 * i: 128 * (i + 1)], KB[i % 2][0:64, 0:128]).inc(
                "v", 1, f"vqowev_{i}")
        g.wait("vqowev_7")
        g.dma_start(out=smallbuf[0:64, 0:D], in_=vqw_h[:, :]).inc("g", 16, "ld_vqw")
        g.wait("w2tr_3_31")
        g.dma_start(out=vqb_row[:], in_=vqb_h[:, :]).inc("g", 16, "ld_vqb")
        for c in range(8):
            if c == 0:
                t.wait("ld_vqw", "vqowev_6", "vqowev_7")
            if c > 1:
                t.wait(f"vqwev_{c-2}")
            t.matmul(KB[c % 2][:, 0:64], smallbuf[0:64, 128 * c: 128 * (c + 1)],
                     eyef[0:64, 0:64], start=True, stop=True).inc("t", 1, f"vqwtr_{c}")
            v.wait(f"vqwtr_{c}")
            if c == 0:
                v.wait("w2sgn_7")
            v.tensor_copy(vqwT[:, 64 * c: 64 * (c + 1)], KB[c % 2][:, 0:64]).inc(
                "v", 1, f"vqwev_{c}")

        for i in range(16):
            q = g if i % 2 == 0 else sy
            if i >= 2:
                q.wait(f"opwtr_{i-2}_1")
            else:
                q.wait("vqwev_7", "w2sgn_7", "w2abs_7", "w2tr_3_31")
            q.dma_start(out=smallbuf[0:NPK, D * (i % 2): D * (i % 2) + D],
                        in_=opw_h[NPK * i: NPK * (i + 1), :]).inc("g", 16, f"ld_opw_{i}")
            t.wait(f"ld_opw_{i}")
            if i == 0:
                t.wait("vqwev_7")
            if i >= 1:
                t.wait(f"opwev_{i-1}_1")
            for c in range(8):
                bank = KB[0] if c < 4 else KB[1]
                ins = t.matmul(bank[:, 126 * (c % 4): 126 * (c % 4) + 126],
                               smallbuf[0:NPK, D * (i % 2) + 128 * c: D * (i % 2) + 128 * (c + 1)],
                               eyef[0:NPK, 0:NPK], start=(c % 4 == 0), stop=(c % 4 == 3))
                if c in (3, 7):
                    ins.inc("t", 1, f"opwtr_{i}_{c // 4}")
            for hf in range(2):
                v.wait(f"opwtr_{i}_{hf}")
                dst = opwT_full[:].rearrange("p (c q) -> p c q", c=8)[
                    :, 4 * hf: 4 * hf + 4, NPK * i: NPK * (i + 1)]
                v.tensor_copy(dst, (KB[0] if hf == 0 else KB[1])[:, 0:4 * NPK].rearrange(
                    "p (c q) -> p c q", q=NPK)).inc("v", 1, f"opwev_{i}_{hf}")
        g.wait("vqwev_7")
        g.dma_start(out=params_s[0:1, 0:NP], in_=opb_h[:, :]).inc("g", 16, "ld_opb")
        v.wait("ld_opb", "s2wr_3_1")
        v.tensor_copy(opb_b[:], params_s[0:1, 0:NP]).inc("v", 1, "opb_ok")

        g.wait("hm_w_1", "s2wr_3_1")
        g.dma_start(out=hm_s[0:2, :], in_=row_scr[2:4, :]).inc("g", 16, "hm_rt")


        for c in range(8):
            if c == 0:
                t.wait("hm_rt", "opwev_15_0", "opwev_15_1")
            if c > 1:
                t.wait(f"hmT_ev_{c-2}")
            t.matmul(KB[c % 2][0:128, 0:2], hm_s[0:2, 128 * c: 128 * (c + 1)],
                     eyef[0:2, 0:2], start=True, stop=True).inc("t", 1, f"hmT_tr_{c}")
            v.wait(f"hmT_tr_{c}")
            v.tensor_copy(hmT[:, 2 * c: 2 * c + 2], KB[c % 2][0:128, 0:2]).inc(
                "v", 1, f"hmT_ev_{c}")
        v.drain()
        v.tensor_copy(hmTb[:], hmT[:]).inc("v", 1, "hmTb_ok")

        for n in range(4):
            for c in range(8):
                if n == 0 and c == 0:
                    t.wait("hmTb_ok", "opb_ok")
                t.matmul(KB[n][0:2, 0:504], hmTb[:, 2 * c: 2 * c + 2],
                         opwT_full[:].rearrange("p (cc q) -> p cc q", cc=8)[
                             :, c, 504 * n: 504 * (n + 1)],
                         start=(c == 0), stop=False)
            t.matmul(KB[n][0:2, 0:504], ones1b[0:1, 0:2],
                     opb_b[0:1, 504 * n: 504 * (n + 1)], start=False, stop=True).inc(
                "t", 1, f"par_{n}")
        for n in range(4):
            v.wait(f"par_{n}")
            if n == 0:
                v.wait("opwev_15_1")
            v.tensor_copy(params_s[0:2, 504 * n: 504 * (n + 1)], KB[n][0:2, 0:504]).inc(
                "v", 1, f"par_ev_{n}")
        for kc in range(16):
            if kc == 0:
                t.wait("par_ev_3")
            t.matmul(KB[4][0:NPK, 2 * kc: 2 * kc + 2],
                     params_s[0:2, NPK * kc: NPK * (kc + 1)], eyef[0:2, 0:2],
                     start=(kc == 0), stop=(kc == 15), skip_group_check=True).inc(
                "t", 1, f"parT_{kc}")
        v.wait("parT_15")
        v.tensor_copy(paramsT[:, :], KB[4][0:NPK, 0:32]).inc("v", 1, "parT_ev")
        v.wait("ldxm_0")
        for jj0 in range(2):
            v.tensor_scalar(xn_tmb[:, D * jj0: D * (jj0 + 1)], xsl(64 + jj0),
                            negm_c[:, jj0: jj0 + 1], rstd_c[:, jj0: jj0 + 1],
                            ALU.add, ALU.mult).inc("v", 1, f"v_xn_0_{jj0}")

        for kc in range(16):
            q = g if kc % 2 == 0 else sy
            if kc >= 4:
                q.wait(f"om_kc_{kc-4}")
            else:
                q.wait("par_3")
            q.dma_start(out=mbuf[:, 4096 * (kc % 4): 4096 * (kc % 4) + 4096],
                        in_=mconst_h[NPK * kc: NPK * (kc + 1), :]).inc(
                "g", 16, f"ld_m_{kc}")
            t.wait(f"ld_m_{kc}")
            if kc == 0:
                t.wait("parT_ev")
            ins = None
            for n in range(8):
                ins = t.matmul(KB[n][0:2, :], paramsT[:, 2 * kc: 2 * kc + 2],
                               mbuf[:, 4096 * (kc % 4) + 512 * n:
                                    4096 * (kc % 4) + 512 * (n + 1)],
                               start=(kc == 0), stop=(kc == 15), skip_group_check=True)
            ins.inc("t", 1, f"om_kc_{kc}")
        for n in range(8):
            if n == 0:
                v.wait("om_kc_15")
            v.tensor_copy(om_stage[0:2, 512 * n: 512 * (n + 1)],
                          KB[n][0:2, :]).inc("v", 1, f"om_ev_{n}")
        g.wait("om_ev_7")
        g.dma_start(out=row_scr[4:12, :].rearrange("(b rr) q -> b (rr q)", b=2),
                    in_=om_stage[0:2, :]).inc("g", 16, "om_w")

        for btch in range(2):
            g.wait("om_w")
            if btch == 1:
                g.wait("G_done_0")
            src = row_scr[4 + 4 * btch: 8 + 4 * btch, :].rearrange(
                "r (q d2) -> (r q) d2", d2=64)
            g.dma_start(out=omg[:, :], in_=src).inc("g", 16, f"ld_omg_{btch}")
            v.wait(f"ld_omg_{btch}")
            v.nop() if False else None
            t.wait(f"ld_omg_{btch}")
            t.matmul(KB[0][0:64, 0:64], omg[:], omg[:], start=True, stop=True).inc(
                "t", 1, f"om2_{btch}")
            v.wait(f"om2_{btch}")
            v.tensor_copy(omg2[:], KB[0][0:64, 0:64]).inc("v", 1, f"om2ev_{btch}")
            t.wait(f"om2ev_{btch}")
            t.matmul(KB[0][0:64, 0:64], omg2[:], omg[:], start=True, stop=True).inc(
                "t", 1, f"om3_{btch}")
            v.wait(f"om3_{btch}")
            v.tensor_copy(omg3[:], KB[0][0:64, 0:64])
            v.drain()
            v.scalar_tensor_tensor(gt1[:], omg2[:], -0.5, omg[:], ALU.mult, ALU.add)
            v.drain()
            v.scalar_tensor_tensor(gt1[:], omg3[:], -1.0 / 6.0, gt1[:], ALU.mult, ALU.add)
            v.drain()
            v.tensor_tensor(gt1[:], gt1[:], eyef[0:64, 0:64], ALU.add)
            v.drain()
            v.tensor_copy(G_b[0:64, 64 * btch: 64 * (btch + 1)], gt1[:]).inc(
                "v", 1, f"G_done_{btch}")

        for c in range(8):
            if c == 0:
                t.wait("om3_1", "hmT_ev_7", "vqwev_7", "consts_v")
            t.matmul(KB[4][0:2, 0:VQD], hmT[:, 2 * c: 2 * c + 2],
                     vqwT[:, 64 * c: 64 * (c + 1)], start=(c == 0), stop=False)
        t.matmul(KB[4][0:2, 0:VQD], onesf2[:, :], vqb_row[:, :],
                 start=False, stop=True).inc("t", 1, "vqin_mm")
        v.wait("vqin_mm")
        v.tensor_copy(vq_in_s[:, :], KB[4][0:2, 0:VQD]).inc("v", 1, "vqin_ev")
        s.wait("vqin_ev")
        s.activation(nvq[:, :], vq_in_s[:, :], AF.Square, accum_out=vnorm[:, :])
        s.drain()
        s.activation(vnorm[:, :], vnorm[:, :], AF.Ln)
        s.drain()
        s.activation(vnorm[:, :], vnorm[:, :], AF.Exp, scale=konst[0:2, 4:5]).inc(
            "s", 1, "vqnorm")
        v.wait("vqnorm")
        v.tensor_scalar_max(vnorm[:, :], vnorm[:, :], 1e-12)
        v.drain()
        v.reciprocal(vnorm[:, :], vnorm[:, :])
        v.drain()
        v.tensor_scalar(nvq[:, :], vq_in_s[:, :], vnorm[:, 0:1], None, ALU.mult).inc(
            "v", 1, "nvq_ok")
        g.wait("ld_vqb", "om_w")
        g.dma_start(out=cb_s[:].rearrange("p (u d2) -> p u d2", u=2),
                    in_=cb_h.rearrange("(u p) d2 -> p u d2", p=128)).inc("g", 16, "ld_cb")
        s.wait("ld_cb")
        s.activation(cbsq[:, 0:VQD], cb_s[:, 0:VQD], AF.Square, accum_out=normt[:, 0:1])
        s.drain()
        s.activation(cbsq[:, 0:VQD], cb_s[:, VQD: 2 * VQD], AF.Square,
                     accum_out=normt[:, 1:2])
        s.drain()
        s.activation(normt[:, :], normt[:, :], AF.Ln)
        s.drain()
        s.activation(normt[:, :], normt[:, :], AF.Exp, scale=konst[:, 4:5]).inc(
            "s", 1, "cbnorm")
        v.wait("cbnorm")
        v.tensor_scalar_max(normt[:, :], normt[:, :], 1e-12)
        v.drain()
        v.reciprocal(normt[:, :], normt[:, :])
        v.drain()
        v.tensor_scalar(ncb[:, 0:VQD], cb_s[:, 0:VQD], normt[:, 0:1], None, ALU.mult)
        v.tensor_scalar(ncb[:, VQD: 2 * VQD], cb_s[:, VQD: 2 * VQD], normt[:, 1:2],
                        None, ALU.mult).inc("v", 1, "ncb_ok")
        t.wait("nvq_ok")
        t.matmul(KB[5][0:64, 0:2], nvq[0:2, 0:VQD], eyef[0:2, 0:2],
                 start=True, stop=True).inc("t", 1, "nvqT_tr")
        v.wait("nvqT_tr")
        v.tensor_copy(nvqT[:, :], KB[5][0:64, 0:2]).inc("v", 1, "nvqT_ev")
        for u in range(2):
            if u == 0:
                t.wait("ncb_ok", "nvqT_ev")
            else:
                t.wait(f"ncbT_ev_0")
            t.matmul(KB[5][0:64, 0:128], ncb[:, VQD * u: VQD * (u + 1)],
                     eyef[:, 0:128], start=True, stop=True).inc("t", 1, f"ncbT_tr_{u}")
            v.wait(f"ncbT_tr_{u}")
            v.tensor_copy(ncbT[0:64, 128 * u: 128 * (u + 1)], KB[5][0:64, 0:128]).inc(
                "v", 1, f"ncbT_ev_{u}")
        t.wait("ncbT_ev_1")
        t.matmul(KB[4][0:2, 0:VQN], nvqT[0:64, 0:2], ncbT[0:64, 0:VQN],
                 start=True, stop=True).inc("t", 1, "scores_mm")
        v.wait("scores_mm")
        v.tensor_copy(scores_s[:, :], KB[4][0:2, 0:VQN])
        v.drain()
        v.reduce_max(vnorm[:, :], scores_s[:, :], axis=AX.X)
        v.drain()
        v.tensor_scalar(onehot[:, :], scores_s[:, :], vnorm[:, 0:1], None,
                        ALU.is_ge).inc("v", 1, "onehot_ok")
        for u in range(2):
            if u == 0:
                t.wait("onehot_ok")
            t.matmul(KB[5][0:128, 2 * u: 2 * u + 2], onehot[0:2, 128 * u: 128 * (u + 1)],
                     eyef[0:2, 0:2], start=(u == 0), stop=(u == 1)).inc(
                "t", 1, f"ohT_tr_{u}")
        v.wait("ohT_tr_1")
        v.tensor_copy(onehotT[:, 0:4], KB[5][0:128, 0:4]).inc("v", 1, "ohT_ev")
        for u in range(2):
            if u == 0:
                t.wait("ohT_ev")
            t.matmul(KB[4][0:2, 0:VQD], onehotT[:, 2 * u: 2 * u + 2],
                     cb_s[:, VQD * u: VQD * (u + 1)], start=(u == 0), stop=(u == 1)).inc(
                "t", 1, f"vqcode_mm_{u}")
        v.wait("vqcode_mm_1")
        v.tensor_copy(vq_code_s[:, :], KB[4][0:2, 0:VQD]).inc("v", 1, "vqcode_ev")
        t.wait("vqcode_ev")
        t.matmul(KB[5][0:64, 0:2], vq_code_s[0:2, 0:VQD], eyef[0:2, 0:2],
                 start=True, stop=True).inc("t", 1, "vqcT_tr")
        v.wait("vqcT_tr")
        v.tensor_copy(vq_codeT[:, :], KB[5][0:64, 0:2]).inc("v", 1, "vqcT_ev")
        g.wait("parT_ev", "s2wr_3_1")
        g.dma_start(out=vqob_row[:, :], in_=vqob_h[:, :]).inc("g", 16, "ld_vqob")
        g.dma_start(out=oob_rep2[:, :], in_=bc(oob_h, 2, [[1, D]])).inc("g", 16, "ld_oob")
        for b in range(2):
            bk0, bk1 = (6, 7) if b == 0 else (4, 5)
            if b == 0:
                t.wait("vqcT_ev", "ld_vqob")
            for half, bk in ((0, bk0), (1, bk1)):
                t.matmul(KB[bk][0:1, :], vq_codeT[:, b: b + 1],
                         vqpowT[0:64, 512 * half: 512 * (half + 1)],
                         start=True, stop=False)
                t.matmul(KB[bk][0:1, :], onesf2[0:1, 0:1],
                         vqob_row[0:1, 512 * half: 512 * (half + 1)],
                         start=False, stop=True).inc("t", 1, f"vqc_mm_{b}_{half}")
            v.wait(f"vqc_mm_{b}_0", f"vqc_mm_{b}_1")
            if b == 0:
                v.wait("ld_ratio", "ld_oob", "b2f_row_ok", "ld_a2rep", "ld_gam",
                       "a1s", "G_done_1", "parT_ev")
            v.scalar_tensor_tensor(C_rowsb[0:1, D * b: D * b + 512],
                                   KB[bk0][0:1, :], ratio_rep[0:1, 0:1],
                                   oob_rep2[0:1, 0:512], ALU.mult, ALU.add)
            v.scalar_tensor_tensor(C_rowsb[0:1, D * b + 512: D * (b + 1)],
                                   KB[bk1][0:1, :], ratio_rep[0:1, 0:1],
                                   oob_rep2[0:1, 512:1024], ALU.mult,
                                   ALU.add).inc("v", 1, f"vqcR_ev_{b}")
        v.drain()
        v.tensor_copy(oowT[64:65, :], C_rowsb[0:1, 0:D]).inc("v", 1, "phaseB_done2")

        if debug:
            g.wait("phaseB_done2")
            for nm, t_sb in [("d_rstd", rstd_c), ("d_negm", negm_c),
                             ("d_alpha1", alpha1_c), ("d_b2f", b2f_row),
                             ("d_hms", hm_s), ("d_Gb", G_b), ("d_Crows", C_rowsb),
                             ("d_vqin", vq_in_s), ("d_scores", scores_s),
                             ("d_vqcode", vq_code_s), ("d_hmT", hmT),
                             ("d_b1c", b1_c), ("d_vqcs", vqc_s)]:
                g.dma_start(out=dbg[nm][:, :], in_=t_sb[:, :]).inc("g", 16)
            g.dma_start(out=dbg["d_a2rep"][:, :], in_=alpha2_rep[0:1, :]).inc("g", 16)
            g.dma_start(out=dbg["d_sqc"][:, :], in_=sq_c[:, :]).inc("g", 16)
            g.dma_start(out=dbg["d_parT"][:, :], in_=paramsT[:, :]).inc("g", 16)
            g.dma_start(out=dbg["d_omg"][:, :], in_=omg[:, :]).inc("g", 16)
            g.dma_start(out=dbg["d_rowscr"][:, :], in_=row_scr[:, :]).inc(
                "g", 16, "dbg_done")

        for tt in range(NTILE + 1):
            def _cast(jj, tt=tt):
                gg = 4 * tt + jj
                li = 64 + gg
                v.wait(f"ldxm_{tt}")
                if tt >= 1 or jj >= 2:
                    pt, pj = (tt, jj - 2) if jj >= 2 else (tt - 1, jj + 2)
                    v.wait(f"t_tr_{pt}_{pj}_1")
                if jj == 0 and tt >= 1:
                    v.wait(f"t_h1_{tt-1}_31")
                v.tensor_scalar(xn_tmb[:, D * (gg % 2): D * (gg % 2 + 1)], xsl(li),
                                negm_c[:, gg: gg + 1], rstd_c[:, gg: gg + 1],
                                ALU.add, ALU.mult).inc("v", 1, f"v_xn_{tt}_{jj}")

            if 0 < tt < NTILE:
                _cast(0)
                _cast(1)
            if tt >= 1:
                te = tt - 1
                bt = te // 8
                TRb, H1b, O2b = banks(te)
                for j in range(4):
                    s.wait(f"v_pre_{te}_{j}")
                    s.drain()
                    s.activation(sqscr[:], pre_buf[:, D * j: D * (j + 1)], AF.Square,
                                 accum_out=ms_t[:, j: j + 1]).inc("s", 1, f"s_sq_{te}_{j}")
                s.drain()
                s.activation(rstd2_t[:], ms_t[:], AF.Ln, bias=konst[:, 2:3],
                             scale=konst[:, 3:4])
                s.drain()
                s.activation(rstd2_t[:], rstd2_t[:], AF.Exp, scale=konst[:, 1:2]).inc(
                    "s", 1, f"s_rstd2_{te}")

                if te == 8:
                    v.wait("t_lie_7_3")
                    v.tensor_copy(oowT[64:65, :], C_rowsb[0:1, D: 2 * D]).inc(
                        "v", 1, "c_swap")
                def _A(j):
                    oj = O2b[j]
                    v.wait(f"t_o2_{te}_{j}")
                    if te >= 1:
                        v.wait(f"st_{te-1}_{j}")
                    v.scalar_tensor_tensor(out_part[:, D * j: D * j + 512],
                                           KB[oj[0]][:, :], 1.0,
                                           alpha2_rep[:, 0:512], ALU.mult, ALU.mult)
                    v.scalar_tensor_tensor(out_part[:, D * j + 512: D * (j + 1)],
                                           KB[oj[1]][:, :], 1.0,
                                           alpha2_rep[:, 512:1024], ALU.mult,
                                           ALU.mult).inc("v", 1, f"v_ab_{te}_{j}")

                def _C(j):
                    li = 64 + 4 * te + j
                    v.wait(f"ldxm_{te}")
                    v.tensor_tensor(out_part[:, D * j: D * (j + 1)],
                                    out_part[:, D * j: D * (j + 1)], xsl(li),
                                    ALU.add).inc("v", 1, f"v_ffnx_{te}_{j}")

                def _D(j):
                    li = 64 + 4 * te + j
                    gg = 4 * te + j
                    v.tensor_scalar(pre_buf[:, D * j: D * (j + 1)], xsl(li),
                                    negm_c[:, gg: gg + 1], rstd_c[:, gg: gg + 1],
                                    ALU.add, ALU.mult)

                def _EF(j):
                    oj = O2b[j]
                    v.wait(f"t_lie_{te}_{j}")
                    v.scalar_tensor_tensor(pre_buf[:, D * j: D * j + 512],
                                           KB[oj[0]][:, :], om_rep[:, 0:1],
                                           pre_buf[:, D * j: D * j + 512],
                                           ALU.mult, ALU.add)
                    v.scalar_tensor_tensor(pre_buf[:, D * j + 512: D * (j + 1)],
                                           KB[oj[1]][:, :], om_rep[:, 0:1],
                                           pre_buf[:, D * j + 512: D * (j + 1)],
                                           ALU.mult, ALU.add).inc("v", 1, f"v_pre_{te}_{j}")

                if te == 0:
                    v.wait("phaseB_done2")
                    if debug:
                        v.wait("dbg_done")
                else:
                    v.wait(f"v_out_{te-1}_3")
                _D(0); _A(0); _D(1); _A(1); _EF(0); _D(2); _A(2); _C(0)
                _EF(1); _D(3); _A(3); _C(1); _EF(2); _C(2); _EF(3); _C(3)
                for j in range(4):
                    oj = O2b[j]
                    t.wait(f"v_ab_{te}_{j}")
                    if te == 8 and j == 0:
                        t.wait("c_swap")
                    t.matmul(KB[oj[0]][:, :], x_rot_bf[0:65, 128 * j: 128 * (j + 1)],
                             oowT[0:65, 0:512], start=True, stop=True)
                    t.matmul(KB[oj[1]][:, :], x_rot_bf[0:65, 128 * j: 128 * (j + 1)],
                             oowT[0:65, 512:1024], start=True, stop=True).inc(
                        "t", 1, f"t_lie_{te}_{j}")

                for j in range(4):
                    g.wait(f"v_out_{te}_{j}")
                    g.dma_start(out=out_h[512 * te + 128 * j: 512 * te + 128 * (j + 1), :],
                                in_=out_part[:, D * j: D * (j + 1)]).inc(
                        "st", 16, f"st_{te}_{j}")

            if tt >= NTILE:
                te = tt - 1
                v.wait(f"s_rstd2_{te}", f"v_ffnx_{te}_3")
                for j in range(4):
                    v.scalar_tensor_tensor(pre_buf[:, D * j: D * (j + 1)],
                                           pre_buf[:, D * j: D * (j + 1)],
                                           rstd2_t[:, j: j + 1], gamma_rep[:],
                                           ALU.mult, ALU.mult)
                v.drain()
                for j in range(4):
                    v.tensor_tensor(out_part[:, D * j: D * (j + 1)],
                                    pre_buf[:, D * j: D * (j + 1)],
                                    out_part[:, D * j: D * (j + 1)],
                                    ALU.add).inc("v", 1, f"v_out_{te}_{j}")
                break

            bt = tt // 8
            TRb, H1b, O2b = banks(tt)
            if tt >= 2:
                for j in range(4):
                    sy.wait(f"v_ffnx_{tt-2}_{j}")
            else:
                if tt == 0:
                    sy.wait("w2sgn_7", "om_kc_15")
                for q in range(4):
                    sy.wait(f"xfA_{56 + 4 * tt + q}")
            sy.dma_start(
                out=x_ring[:, D * ((4 * tt) % 8): D * ((4 * tt) % 8) + 4 * D].rearrange(
                    "p (gq d2) -> p gq d2", gq=4),
                in_=x_h[512 * tt: 512 * (tt + 1), :].rearrange(
                    "(gq p) d2 -> p gq d2", p=128)).inc("x", 16, f"ldxm_{tt}")

            for bb in range(16):
                gb = 16 * tt + bb
                if gb >= 3:
                    g.wait(f"s2f_{gb-3}")
                else:
                    g.wait("s2wr_3_1", "opwtr_15_1")
                sl = gb % 3
                g.dma_start(
                    out=s2r[:, 2048 * sl: 2048 * (sl + 1)].rearrange(
                        "p (kk d2) -> p kk d2", kk=2),
                    in_=s2t_scr.rearrange("(kk p) d2 -> p kk d2", p=128)[
                        :, 2 * bb: 2 * bb + 2, :]).inc("s2", 16, f"s2ld_{tt}_{bb}")

            for jj in range(4):
                gg = 4 * tt + jj
                if jj == 1:
                    _cast(2)
                if jj == 2:
                    _cast(3)
                for hf in range(2):
                    v.wait(f"t_tr_{tt}_{jj}_{hf}")
                    dst = xn_fm[:].rearrange("p (c q) -> p c q", c=8)[
                        :, 4 * hf: 4 * hf + 4, 128 * jj: 128 * (jj + 1)]
                    v.tensor_copy(dst, (KB[TRb[0]] if hf == 0 else KB[TRb[1]])[:].rearrange(
                        "p (c q) -> p c q", c=4)).inc("v", 1, f"v_tev_{tt}_{jj}_{hf}")
            v.wait(f"t_xr_{tt}")
            v.tensor_copy(x_rot_bf[0:64, :], KB[TRb[0]][0:64, :]).inc("v", 1, f"v_xrev_{tt}")
            if tt >= 1:
                te = tt - 1
                v.wait(f"s_rstd2_{te}", f"v_ffnx_{te}_3")
                for j in range(4):
                    v.scalar_tensor_tensor(pre_buf[:, D * j: D * (j + 1)],
                                           pre_buf[:, D * j: D * (j + 1)],
                                           rstd2_t[:, j: j + 1], gamma_rep[:],
                                           ALU.mult, ALU.mult)
                v.drain()
                for j in range(4):
                    v.tensor_tensor(out_part[:, D * j: D * (j + 1)],
                                    pre_buf[:, D * j: D * (j + 1)],
                                    out_part[:, D * j: D * (j + 1)],
                                    ALU.add).inc("v", 1, f"v_out_{te}_{j}")

            for jj in range(4):
                gg = 4 * tt + jj
                t.wait(f"v_xn_{tt}_{jj}")
                if jj == 0:
                    if tt == 0:
                        t.wait("G_done_1", "om_ev_1")
                    else:
                        t.wait(f"v_pre_{tt-1}_0")
                if jj >= 1:
                    t.wait(f"v_tev_{tt}_{jj-1}_0", f"v_tev_{tt}_{jj-1}_1")
                for c in range(8):
                    bank = KB[TRb[0]] if c < 4 else KB[TRb[1]]
                    ins = t.matmul(bank[:, 128 * (c % 4): 128 * (c % 4 + 1)],
                                   xn_tmb[:, D * (gg % 2) + 128 * c:
                                          D * (gg % 2) + 128 * (c + 1)],
                                   eyeb[:], start=(c % 4 == 0), stop=(c % 4 == 3))
                    if c in (3, 7):
                        ins.inc("t", 1, f"t_tr_{tt}_{jj}_{c // 4}")
            t.wait(f"v_tev_{tt}_3_0", f"v_tev_{tt}_3_1")
            t.matmul(KB[TRb[0]][0:64, :], G_b[0:64, 64 * bt: 64 * (bt + 1)],
                     xn_fm[0:64, 0:512], start=True, stop=True).inc("t", 1, f"t_xr_{tt}")

            for m in range(32):
                bank = KB[H1b[0]] if (m % 2) == 0 else KB[H1b[1]]
                if m == 0:
                    if tt == 0:
                        t.wait("om_ev_2", "om_ev_3")
                    else:
                        t.wait(f"v_pre_{tt-1}_1")
                if m >= 2:
                    t.wait(f"s_gel_{tt}_{m-2}")
                for c in range(8):
                    ins = t.matmul(bank[:, :],
                                   s1T[:, H * c + 128 * m: H * c + 128 * (m + 1)],
                                   xn_fm[:, 512 * c: 512 * (c + 1)],
                                   start=(c == 0), stop=(c == 7))
                ins.inc("t", 1, f"t_h1_{tt}_{m}")

            for k in range(32):
                gb = 16 * tt + k // 2
                if k % 2 == 0:
                    t.wait(f"s2ld_{tt}_{k//2}")
                t.wait(f"s_gel_{tt}_{k}")
                if k == 0:
                    if tt == 0:
                        t.wait("vqcR_ev_0", "vqcR_ev_1")
                    else:
                        t.wait(f"v_pre_{tt-1}_2", f"v_pre_{tt-1}_3")
                    t.wait(f"v_xrev_{tt}", f"s_gel_{tt}_31")
                ins0 = None
                for j in range(4):
                    oj = O2b[j]
                    for hh in range(2):
                        ins = t.matmul(
                            KB[oj[hh]][:, :],
                            g_all[:, 512 * k + 128 * j: 512 * k + 128 * (j + 1)],
                            s2r[:, 2048 * (gb % 3) + 1024 * (k % 2) + 512 * hh:
                                2048 * (gb % 3) + 1024 * (k % 2) + 512 * (hh + 1)],
                            start=(k == 0), stop=False)
                        if j == 3 and hh == 0:
                            ins0 = ins
                ins0.inc("t", 1, f"t_o2k_{tt}_{k}")
                if k % 2 == 1:
                    ins.inc("s2f", 1, f"s2f_{gb}")
            for j in range(4):
                oj = O2b[j]
                t.matmul(KB[oj[0]][:, :], ones1b[0:1, 0:128], b2f_row[0:1, 0:512],
                         start=False, stop=True)
                t.matmul(KB[oj[1]][:, :], ones1b[0:1, 0:128], b2f_row[0:1, 512:1024],
                         start=False, stop=True).inc("t", 1, f"t_o2_{tt}_{j}")

            if debug and tt == 1 and False:
                g.wait("t_h1_0_31", "s_gel_0_31", "v_xrev_0", "s2ld_0_0")
                g.dma_start(out=dbg["d_xnfm"][:, :], in_=xn_fm[:, :]).inc("g", 16)
                g.dma_start(out=dbg["d_gall"][:, :], in_=g_all[:, :]).inc("g", 16)
                g.dma_start(out=dbg["d_s1T"][:, :], in_=s1T[:, 0:H]).inc("g", 16)
                g.dma_start(out=dbg["d_s2r"][:, :], in_=s2r[:, 0:2048]).inc("g", 16)
                g.dma_start(out=dbg["d_xrot"][:, :], in_=x_rot_bf[:, :]).inc(
                    "g", 16, "dbg2_done")
            for m in range(32):
                s.wait(f"t_h1_{tt}_{m}")
                if tt >= 1:
                    s.wait(f"t_o2k_{tt-1}_{m}")
                if debug and tt == 1 and m == 0 and False:
                    s.wait("dbg2_done")
                bank = KB[H1b[0]] if (m % 2) == 0 else KB[H1b[1]]
                s.activation(g_all[:, 512 * m: 512 * (m + 1)], bank[:, :], AF.Gelu,
                             bias=b1_c[:, m: m + 1], scale=alpha1_c[:, m: m + 1]).inc(
                    "s", 1, f"s_gel_{tt}_{m}")

        g.wait(f"st_{NTILE-1}_0", f"st_{NTILE-1}_1", f"st_{NTILE-1}_2",
               f"st_{NTILE-1}_3")

    E.reset_pass()
    program(E)
    missing = E.waited - set(E.events)
    assert not missing, f"waits on undefined events: {sorted(missing)[:10]}"
    semv = {}
    ptr = {k: 0 for k in E.streams}
    progress = True
    while progress:
        progress = False
        for k, st_ in E.streams.items():
            while ptr[k] < len(st_):
                it = st_[ptr[k]]
                if it[0] == "wait":
                    sem, val = E.events[it[1]]
                    if semv.get(sem, 0) < val:
                        break
                elif it[0] == "waitv":
                    if semv.get(it[1], 0) < it[2]:
                        break
                else:
                    semv[it[2] if False else it[1]] = semv.get(it[1], 0) + it[2]
                ptr[k] += 1
                progress = True
    stuck = {k: (ptr[k], len(E.streams[k]),
                 E.streams[k][ptr[k]] if ptr[k] < len(E.streams[k]) else None)
             for k in E.streams if ptr[k] < len(E.streams[k])}
    assert not stuck, f"DEADLOCK: {stuck}"
    E.counting = False
    with nc.Block() as block:
        @block.tensor
        def _(eng):
            E.reset_pass()
            E.cur, E.handle = "t", eng
            program(E)

        @block.vector
        def _(eng):
            E.reset_pass()
            E.cur, E.handle = "v", eng
            program(E)

        @block.scalar
        def _(eng):
            E.reset_pass()
            E.cur, E.handle = "s", eng
            program(E)

        @block.sync
        def _(eng):
            E.reset_pass()
            E.cur, E.handle = "sy", eng
            program(E)

        @block.gpsimd
        def _(eng):
            E.reset_pass()
            E.cur, E.handle = "g", eng
            program(E)

    return nc, ctx


_TRIU_R, _TRIU_C = np.triu_indices(OD, k=1)


def _mconst():
    m = np.zeros((NP, OD * OD), dtype=ml_dtypes.bfloat16)
    r = np.arange(NP)
    m[r, OD * _TRIU_R + _TRIU_C] = 1.0
    m[r, OD * _TRIU_C + _TRIU_R] = -1.0
    return m


def kernel(**inputs):
    def f32(key):
        return np.ascontiguousarray(np.asarray(inputs[key], dtype=np.float32))

    x = f32("L0_attn_x").reshape(B * L, D)
    shared = {
        "w1": f32("L0_ffn_w1"),
        "w2": f32("L0_ffn_w2"),
        "b1": f32("L0_ffn_b1").reshape(1, H),
        "b2": f32("L0_ffn_b2").reshape(1, D),
        "opw": f32("L0_attn_omega_proj_w"),
        "opb": f32("L0_attn_omega_proj_b").reshape(1, NP),
        "oow": f32("L0_attn_omega_out_w"),
        "oob": f32("L0_attn_omega_out_b").reshape(1, D),
        "vqw": f32("L0_attn_vq_proj_in_w"),
        "vqb": f32("L0_attn_vq_proj_in_b").reshape(1, VQD),
        "vqow": f32("L0_attn_vq_proj_out_w"),
        "vqob": f32("L0_attn_vq_proj_out_b").reshape(1, D),
        "cb": f32("L0_attn_codebook"),
        "omix": f32("L0_attn_omega_mix").reshape(1, 1),
        "vmix": f32("L0_attn_vq_mix").reshape(1, 1),
        "gam": f32("L0_attn_norm_w").reshape(1, D),
        "mconst": _mconst(),
        "eyef": np.eye(128, dtype=np.float32),
        "eyeb": np.eye(128, dtype=ml_dtypes.bfloat16),
    }
    in_maps = []
    for c in range(NCORE):
        m = dict(shared)
        m["x"] = np.ascontiguousarray(x[TCORE * c: TCORE * (c + 1), :])
        in_maps.append(m)

    nc, ctx = build_program()
    res = run_bass_kernel_spmd(nc, in_maps, core_ids=list(range(NCORE)))
    out = np.concatenate([np.asarray(r["out"]) for r in res.results], axis=0)
    return out.reshape(B, L, D).astype(np.float32)
